# version 07
# baseline (speedup 1.0000x reference)
import numpy as np
import ml_dtypes
import concourse.bass as bass
import concourse.mybir as mybir
from concourse.bass_utils import run_bass_kernel_spmd

F32 = mybir.dt.float32
BF16 = mybir.dt.bfloat16
I32 = mybir.dt.int32
AF = mybir.ActivationFunctionType
ALU = mybir.AluOpType
AX = mybir.AxisListType

D = 1024
NH = 16
HD = 64
NE = 12
HF = 4096
L = 4096
OWN = 2048
HALF = 1024
NT = 8
DC = 8
HC = 32
NCORES = 8
SHIFT = -14.0
BF = ml_dtypes.bfloat16


class Em:
    ENGS = ("sync", "tensor", "vector", "scalar", "gpsimd")

    def __init__(self):
        self.ev = {k: [] for k in self.ENGS}
        self.cnt = {s: 0 for s in ("q_sp", "q_pl", "q_spin", "pe", "act", "dve", "pl")}

    def op(self, eng, build, waits=(), inc=None):
        w = tuple(waits)
        self.ev[eng].append((build, w, inc))
        if inc is not None:
            self.cnt[inc[0]] += inc[1]
            return self.cnt[inc[0]]
        return None

    def raw(self, eng, fn, bump=()):
        self.ev[eng].append((fn, None, None))
        for s, a in bump:
            self.cnt[s] += a
        return {s: self.cnt[s] for s, _ in bump}


def _sel_matrix():
    m = np.zeros((12, 12 * 128), dtype=np.float32)
    for e in range(12):
        m[e, e * 128:(e + 1) * 128] = 1.0
    return m.astype(BF)


def _prot_matrix():
    m = np.zeros((128, 128), dtype=np.float32)
    for blk in range(2):
        o = blk * 64
        for i in range(32):
            m[o + 2 * i + 1, o + 2 * i] = -1.0
            m[o + 2 * i, o + 2 * i + 1] = 1.0
    return m.astype(BF)


def _rope_tab(positions):
    inv = 1.0 / (10000.0 ** (np.arange(0, HD, 2, dtype=np.float64) / HD))
    fr = positions[None, :].astype(np.float64) * np.repeat(inv, 2)[:, None]
    return np.concatenate([np.cos(fr), np.sin(fr)], axis=0).astype(np.float32)


def build_program(stage="full"):
    nc = bass.Bass("TRN2", target_bir_lowering=False, debug=False)
    em = Em()

    xin = nc.dram_tensor("xin", [OWN, D], F32, kind="ExternalInput").ap()
    Wq = nc.dram_tensor("Wq", [D, D], F32, kind="ExternalInput").ap()
    Wk = nc.dram_tensor("Wk", [D, D], F32, kind="ExternalInput").ap()
    Wv = nc.dram_tensor("Wv", [D, D], F32, kind="ExternalInput").ap()
    Wp = nc.dram_tensor("Wp", [D, D], F32, kind="ExternalInput").ap()
    bqkvp = nc.dram_tensor("bqkvp", [1, 4 * D], F32, kind="ExternalInput").ap()
    Wg = nc.dram_tensor("Wg", [D, NE], F32, kind="ExternalInput").ap()
    Wup = nc.dram_tensor("Wup", [NE, D, HF], F32, kind="ExternalInput").ap()
    bup = nc.dram_tensor("bupb", [NE, HF], F32, kind="ExternalInput").ap()
    Wdn = nc.dram_tensor("Wdn", [NE, HF, D], F32, kind="ExternalInput").ap()
    bdn = nc.dram_tensor("bdnb", [NE, D], F32, kind="ExternalInput").ap()
    ident_i = nc.dram_tensor("identi", [128, 128], BF16, kind="ExternalInput").ap()
    prot_i = nc.dram_tensor("proti", [128, 128], BF16, kind="ExternalInput").ap()
    sel_i = nc.dram_tensor("seli", [12, 12 * 128], BF16, kind="ExternalInput").ap()
    ones_i = nc.dram_tensor("onesi", [1, 512], BF16, kind="ExternalInput").ap()
    qcos_i = nc.dram_tensor("qcosi", [128, OWN], BF16, kind="ExternalInput").ap()
    qsin_i = nc.dram_tensor("qsini", [128, OWN], BF16, kind="ExternalInput").ap()
    kcos_i = nc.dram_tensor("kcosi", [128, L], BF16, kind="ExternalInput").ap()
    ksin_i = nc.dram_tensor("ksini", [128, L], BF16, kind="ExternalInput").ap()
    nonce_i = nc.dram_tensor("noncei", [1, 128], I32, kind="ExternalInput").ap()
    flagv_i = nc.dram_tensor("flagvi", [1, 128], I32, kind="ExternalInput").ap()

    yout = nc.dram_tensor("yout", [OWN, D], F32, kind="ExternalOutput").ap()

    xscr1 = nc.dram_tensor("xscr1", [OWN, D], F32, kind="Internal").ap()
    xscr2 = nc.dram_tensor("xscr2", [OWN, D], F32, kind="Internal").ap()
    xch = nc.dram_tensor("xch", [2, 128, DC, OWN], BF16, kind="Internal",
                         addr_space="Shared").ap()
    Ffl = nc.dram_tensor("fl", [2, 128], I32, kind="Internal",
                         addr_space="Shared").ap()

    SB_BASE = 16544
    SBUF_LIMIT = 229376
    _spans = []

    def sb(name, p, cols, dtype, off, group):
        off = off + SB_BASE
        end = off + cols * mybir.dt.size(dtype)
        assert end <= SBUF_LIMIT, (name, off, end)
        _spans.append((name, off, end, group))
        return nc.alloc_sbuf_tensor_at(name, [p, cols], dtype, offset=off)

    ident = sb("ident", 128, 128, BF16, 0, "P")
    prot = sb("prot", 128, 128, BF16, 256, "P")
    sel = sb("sel", 12, 12 * 128, BF16, 512, "P")
    ones = sb("ones", 1, 512, BF16, 3584, "P")
    qkvp_b = sb("qkvpb", 1, 4 * D, BF16, 4608, "P")
    wg_sb = sb("wg", 128, DC * NE, BF16, 12800, "P")
    bup_sb = sb("bupsb", 128, HC * NE, F32, 12992, "P")
    bdn_bf = sb("bdnbf", 12, D, BF16, 14528, "P")
    cwT_bf = sb("cwT", 12, HALF, BF16, 16576, "P")
    cw_all = sb("cwall", 128, NT * NE, F32, 18624, "P")
    cw_bf = sb("cwbf", 128, NT * NE, BF16, 19008, "P")
    lg_t = sb("lg", 128, NE, F32, 19200, "P")
    msk_t = sb("msk", 128, NE, F32, 19264, "P")
    tmp_t = sb("tmp", 128, NE, F32, 19328, "P")
    l2_t = sb("l2", 128, NE, F32, 19392, "P")
    smalls = sb("smalls", 128, 8, F32, 19456, "P")
    m1_t = smalls[:, 0:1]
    m1n_t = smalls[:, 1:2]
    mm_t = smalls[:, 2:3]
    rs_t = smalls[:, 3:4]
    sq_m = smalls[:, 4:5]
    var_t = smalls[:, 5:6]
    negc = smalls[:, 6:7]
    nonce_sb = sb("noncesb", 1, 8, I32, 19488, "P")
    lnst = sb("lnst", 128, 24, F32, 19520, "P")
    lg2_all = sb("lg2all", 128, 4 * NE, F32, 19616, "P")
    sel_all = sb("selall", 128, 4 * NE, F32, 19808, "P")
    flagv_sb = sb("flagvsb", 1, 128, I32, 20000, "P")
    flag_rd = sb("flagrd", 1, 256, I32, 20512, "P")
    cwrep = sb("cwrep", 128, HALF, BF16, 21536, "P")

    x_sb = sb("x", 128, NT * D, F32, 23584, "F")
    xnT = sb("xnT", 128, DC * HALF, BF16, 56352, "F")
    wup_bf = [sb(f"wup{i}", 128, DC * 256, BF16, 72736 + 4096 * i, "F")
              for i in range(2)]
    h_sb = sb("h", 128, HC * HALF, BF16, 80928, "F")
    wdn_sb = sb("wdn", 128, HC * D, BF16, 146464, "F")

    bup_t = sb("bupt", 12, HF, F32, 146464, "S")
    identf = sb("identf", 128, 128, F32, 162848, "S")

    xnt_A = (sb("xntA0", 128, D, BF16, 80928, "S"),
             sb("xntA1", 128, D, BF16, 87072, "S"))
    sq_A = sb("sqA", 128, D, F32, 82976, "S")
    xnt_B = (sb("xntB0", 128, D, BF16, 121888, "S"),
             sb("xntB1", 128, D, BF16, 123936, "S"))
    sq_B = sb("sqB", 128, D, F32, 125984, "S")

    xn1T = sb("xn1T", 128, DC * L, BF16, 56352, "A")
    qcos = sb("qcos", 128, OWN, BF16, 121888, "A")
    qsin = sb("qsin", 128, OWN, BF16, 125984, "A")
    kcos = sb("kcos", 128, L, BF16, 130080, "A")
    ksin = sb("ksin", 128, L, BF16, 138272, "A")
    kT_h = [sb(f"kT{i}", 64, L, BF16, 146464 + 8192 * i, "A") for i in range(2)]
    qT_h = [sb(f"qT{i}", 64, OWN, BF16, 162848 + 4096 * i, "A") for i in range(2)]
    v_hp = sb("vhp", 128, 32 * 130, BF16, 171040, "A")
    pt_sb = [sb(f"pt{i}", 128, 1024, BF16, 179360 + 2048 * i, "A") for i in range(2)]
    pt_sb2 = [sb(f"ptb{i}", 128, 1024, BF16, 183456 + 2048 * i, "A") for i in range(2)]
    g_sb = sb("gsb", 128, 512, BF16, 183456, "A")
    t1_sb = sb("t1", 128, 512, F32, 184480, "A")
    t2_sb = sb("t2", 128, 512, F32, 186528, "A")
    wqkv_bf = [sb(f"wqkv{i}", 128, DC * 128, BF16, 188576 + 2048 * i, "A")
               for i in range(6)]
    wp_bf = [sb(f"wp{i}", 128, D, BF16, 200864 + 2048 * i, "A") for i in range(2)]
    recip_sb = sb("recip", 1, 512, F32, 204960, "A")
    rep_sb = sb("repsb", 64, 512, F32, 207008, "A")
    onesf = sb("onesf", 1, 64, F32, 209056, "A")
    attn_oT = sb("attnoT", 128, DC * OWN, BF16, 23584, "A")
    x2_sb = sb("x2", 128, NT * D, F32, 56352, "J")
    xst = [sb(f"xst{i}", 128, D, F32, 89120 + 4096 * i, "J") for i in range(2)]

    PA = nc.alloc_psum_tensor("psA", [128, 1024], F32)
    PB = nc.alloc_psum_tensor("psB", [128, 1024], F32)
    PC = nc.alloc_psum_tensor("psC", [128, 1024], F32)
    PD = nc.alloc_psum_tensor("psD", [128, 1024], F32)
    PBv = [PB[:, 0:512].bitcast(BF16), PB[:, 512:1024].bitcast(BF16)]

    def dma(q, dst, src, waits=()):
        sem = "q_sp" if q == "sync" else "q_pl"
        return em.op(q, lambda e, d=dst, s=src: e.dma_start(d, s),
                     waits=waits, inc=(sem, 16))

    def vop(build, waits=()):
        return em.op("vector", build, waits=waits, inc=("dve", 1))

    def sop(build, waits=()):
        return em.op("scalar", build, waits=waits, inc=("act", 1))

    def top(build, waits=(), inc=True):
        return em.op("tensor", build, waits=waits,
                     inc=("pe", 1) if inc else None)

    def x_view(t):
        return t[:].rearrange("p (tt d) -> p tt d", tt=NT)

    def xdram_half(t, half):
        return t[half * HALF:(half + 1) * HALF].rearrange("(tt p) d -> p tt d", p=128)

    c_ident = dma("sync", ident[:], ident_i[:])
    dma("sync", prot[:], prot_i[:])
    c_prot = em.cnt["q_sp"]
    dma("sync", sel[:], sel_i[:])
    dma("sync", ones[:], ones_i[:])
    dma("sync", nonce_sb[:], nonce_i[0:1, 0:8])
    dma("sync", flagv_sb[:], flagv_i[:])
    c_bupt = dma("sync", bup_t[:], bup[:, :])
    c_consts = em.cnt["q_sp"]
    c_wg = dma("gpsimd", wg_sb[:].rearrange("p (dc e) -> p dc e", dc=DC),
               Wg.rearrange("(dc p) e -> p dc e", p=128))
    c_bq = dma("gpsimd", qkvp_b[:], bqkvp[:])
    c_bdn = dma("gpsimd", bdn_bf[:], bdn[:, :])
    c_x0 = dma("sync", x_view(x_sb), xdram_half(xin, 0))
    c_negc = vop(lambda v: v.memset(negc, SHIFT))
    c_idf = vop(lambda v: v.tensor_copy(identf[:], ident[:]),
                waits=[("q_sp", c_ident)])
    bup_ev = {}
    for hc in range(HC):
        reg = PA[:, 0:512] if hc % 2 == 0 else PA[:, 512:1024]
        w = [("q_sp", c_bupt), ("dve", c_idf)]
        if hc >= 2:
            w.append(("dve", bup_ev[hc - 2]))
        tb = top(lambda t, hc=hc, reg=reg: t.transpose(
            reg[:, 0:12], bup_t[0:12, hc * 128:(hc + 1) * 128],
            identf[0:12, 0:12]), waits=w)
        bup_ev[hc] = vop(lambda v, hc=hc, reg=reg: v.tensor_copy(
            bup_sb[:, hc * NE:(hc + 1) * NE], reg[:, 0:12]),
            waits=[("pe", tb)])
    setup_done_dve = em.cnt["dve"]
    setup_done_pe = em.cnt["pe"]

    PAv8 = PA[:, 0:512].bitcast(BF16)
    PBv8 = PB[:, 0:512].bitcast(BF16)
    m_all = lnst[:, 0:8]
    rstd_all = lnst[:, 8:16]
    v_all = lnst[:, 16:24]

    def emit_ln(xbuf, xload_tok, dest_view, xnt_pair, sq_s):
        psum_guard = ("act", em.cnt["act"])
        last = 0
        for ts in range(NT):
            xt = xbuf[:, ts * D:(ts + 1) * D]
            c1 = vop(lambda v, xt=xt, ts=ts: v.reduce_sum(
                lnst[:, ts:ts + 1], xt, axis=AX.X), waits=list(xload_tok))
            csq = vop(lambda v, xt=xt: v.tensor_mul(sq_s[:], xt, xt),
                      waits=[("dve", last)] if last else list(xload_tok))
            last = vop(lambda v, ts=ts: v.reduce_sum(
                lnst[:, 16 + ts:17 + ts], sq_s[:], axis=AX.X),
                waits=[("dve", csq)])
        c4 = vop(lambda v: v.tensor_scalar_mul(m_all, m_all, 1.0 / D),
                 waits=[("dve", last)])
        c5 = vop(lambda v: v.tensor_scalar_mul(v_all, v_all, 1.0 / D),
                 waits=[("dve", c4)])
        c6 = vop(lambda v: v.tensor_mul(rstd_all, m_all, m_all),
                 waits=[("dve", c5)])
        c7 = vop(lambda v: v.tensor_tensor(v_all, v_all, rstd_all, ALU.subtract),
                 waits=[("dve", c6)])
        c8 = vop(lambda v: v.tensor_scalar_add(v_all, v_all, 1e-5),
                 waits=[("dve", c7)])
        c9 = vop(lambda v: v.reciprocal(rstd_all, v_all), waits=[("dve", c8)])
        c10 = sop(lambda s: s.activation(rstd_all, rstd_all, AF.Sqrt),
                  waits=[("dve", c9)])
        tr_ev = {}
        tr_sv = {}
        norm_ev = {}
        last_xread = 0
        for ts in range(NT):
            xt = xbuf[:, ts * D:(ts + 1) * D]
            xnt_s = xnt_pair[ts % 2]
            wn = [("act", c10)]
            if ts >= 2:
                wn.append(("pe", tr_ev[ts - 2]))
            norm_ev[ts] = vop(lambda v, xt=xt, ts=ts, xnt_s=xnt_s: v.tensor_scalar(
                xnt_s[:], xt, lnst[:, ts:ts + 1], lnst[:, 8 + ts:9 + ts],
                ALU.subtract, ALU.mult), waits=wn)
            last_xread = norm_ev[ts]
            bank = PAv8 if ts % 2 == 0 else PBv8
            for dcc in range(DC):
                w = [("dve", norm_ev[ts]), ("q_sp", c_ident)]
                if dcc == 0:
                    w.append(psum_guard if ts < 2 else ("act", tr_sv[ts - 2]))
                top(lambda t, dcc=dcc, bank=bank, xnt_s=xnt_s: t.transpose(
                    bank[:, dcc * 128:(dcc + 1) * 128],
                    xnt_s[:, dcc * 128:(dcc + 1) * 128],
                    ident[:]), waits=w if dcc == 0 else (),
                    inc=(dcc == DC - 1))
            tr_ev[ts] = em.cnt["pe"]
            dv = dest_view(ts)
            tr_sv[ts] = sop(lambda s, dv=dv, bank=bank: s.copy(
                dv, bank[:].rearrange("p (dc n) -> p dc n", dc=DC)),
                waits=[("pe", tr_ev[ts])])
        return tr_sv[NT - 1], last_xread

    def emit_gate(xnT_tok):
        ev = {}
        psum_guard = ("dve", em.cnt["dve"])
        post = {}
        last_exp = 0
        for ts in range(NT):
            reg = PC[:, 0:512] if ts % 2 == 0 else PC[:, 512:1024]
            for dcc in range(DC):
                w = []
                if dcc == 0:
                    w = [("act", xnT_tok), ("q_pl", c_wg)]
                    if ts >= 2:
                        w.append(("dve", ev[ts - 2]))
                    else:
                        w.append(psum_guard)
                top(lambda t, ts=ts, dcc=dcc, reg=reg: t.matmul(
                    reg[:, 0:NE],
                    xnT[:, dcc * HALF + ts * 128: dcc * HALF + (ts + 1) * 128],
                    wg_sb[:, dcc * NE:(dcc + 1) * NE],
                    start=(dcc == 0), stop=(dcc == DC - 1)),
                    waits=w, inc=(dcc == DC - 1))
            tgate = em.cnt["pe"]
            sl = (ts % 4) * NE
            c0 = vop(lambda v, reg=reg: v.tensor_copy(lg_t[:], reg[:, 0:NE]),
                     waits=[("pe", tgate)])
            c1 = vop(lambda v: v.reduce_max(m1_t, lg_t[:], axis=AX.X),
                     waits=[("dve", c0)])
            c2 = vop(lambda v: v.tensor_single_scalar(msk_t[:], lg_t[:], m1_t,
                                                      ALU.is_lt),
                     waits=[("dve", c1)])
            c3 = vop(lambda v: v.tensor_scalar(tmp_t[:], msk_t[:], 1e30, -1e30,
                                               ALU.mult, ALU.add),
                     waits=[("dve", c2)])
            c4 = vop(lambda v: v.tensor_mul(l2_t[:], lg_t[:], msk_t[:]),
                     waits=[("dve", c2)])
            c5 = vop(lambda v: v.tensor_add(l2_t[:], l2_t[:], tmp_t[:]),
                     waits=[("dve", c3), ("dve", c4)])
            c6 = vop(lambda v: v.reduce_max(mm_t, l2_t[:], axis=AX.X),
                     waits=[("dve", c5)])
            c7 = vop(lambda v: v.tensor_single_scalar(msk_t[:], l2_t[:], mm_t,
                                                      ALU.is_lt),
                     waits=[("dve", c6)])
            c8 = vop(lambda v: v.tensor_scalar(tmp_t[:], msk_t[:], 1e30, -1e30,
                                               ALU.mult, ALU.add),
                     waits=[("dve", c7)])
            c9 = vop(lambda v: v.tensor_mul(l2_t[:], l2_t[:], msk_t[:]),
                     waits=[("dve", c7)])
            c10 = vop(lambda v: v.tensor_add(l2_t[:], l2_t[:], tmp_t[:]),
                      waits=[("dve", c8), ("dve", c9)])
            c11 = vop(lambda v: v.reduce_max(mm_t, l2_t[:], axis=AX.X),
                      waits=[("dve", c10)])
            wex = [("act", last_exp)] if last_exp else []
            c12 = vop(lambda v, sl=sl: v.tensor_single_scalar(
                sel_all[:, sl:sl + NE], lg_t[:], mm_t, ALU.is_ge),
                waits=[("dve", c11)] + wex)
            c13 = vop(lambda v, sl=sl: v.tensor_single_scalar(
                lg2_all[:, sl:sl + NE], lg_t[:], m1_t, ALU.subtract),
                waits=[("dve", c1)] + wex)
            ev[ts] = max(c12, c13)
            if ts % 4 == 3:
                ce = sop(lambda s: s.activation(lg2_all[:], lg2_all[:], AF.Exp),
                         waits=[("dve", max(ev[ts - 3], ev[ts - 2],
                                            ev[ts - 1], ev[ts]))])
                last_exp = ce
                for t2 in range(ts - 3, ts + 1):
                    sl2 = (t2 % 4) * NE
                    d1 = vop(lambda v, sl2=sl2: v.tensor_tensor(
                        lg2_all[:, sl2:sl2 + NE], lg2_all[:, sl2:sl2 + NE],
                        sel_all[:, sl2:sl2 + NE], ALU.mult),
                        waits=[("act", ce)])
                    d2 = vop(lambda v, sl2=sl2: v.reduce_sum(
                        mm_t, lg2_all[:, sl2:sl2 + NE], axis=AX.X),
                        waits=[("dve", d1)])
                    d3 = vop(lambda v: v.reciprocal(rs_t, mm_t),
                             waits=[("dve", d2)])
                    d4 = vop(lambda v: v.tensor_scalar_mul(rs_t, rs_t, 0.5),
                             waits=[("dve", d3)])
                    post[t2] = vop(lambda v, t2=t2, sl2=sl2: v.tensor_single_scalar(
                        cw_all[:, t2 * NE:(t2 + 1) * NE],
                        lg2_all[:, sl2:sl2 + NE], rs_t, ALU.mult),
                        waits=[("dve", d4)])
        ccast = vop(lambda v: v.tensor_copy(cw_bf[:], cw_all[:]),
                    waits=[("dve", post[NT - 1])])
        w0 = [("dve", ccast), ("act", xnT_tok)]
        for ts in range(NT):
            top(lambda t, ts=ts: t.transpose(
                PAv8[0:12, ts * 128:(ts + 1) * 128],
                cw_bf[:, ts * NE:(ts + 1) * NE], ident[:]),
                waits=w0 if ts == 0 else (), inc=(ts == NT - 1))
        tcw = em.cnt["pe"]
        return vop(lambda v: v.tensor_copy(cwT_bf[0:12, :], PAv8[0:12, :]),
                   waits=[("pe", tcw)])

    state = {"last_dn_evac": 0, "last_dn_evac_PD": 0, "wup_last": [0, 0],
             "gelu_last": [0, 0], "dn_evac6": 0}

    def emit_cwrep(e, cw_tok, pc_guard):
        w0 = [("dve", cw_tok), ("q_sp", c_consts)]
        if pc_guard:
            w0.append(("dve", pc_guard))
        top(lambda t, e=e: t.matmul(
            PC[:, 0:512], sel[:, e * 128:(e + 1) * 128], cwT_bf[:, 0:512],
            start=True, stop=True), waits=w0, inc=False)
        t2 = top(lambda t, e=e: t.matmul(
            PC[:, 512:1024], sel[:, e * 128:(e + 1) * 128],
            cwT_bf[:, 512:1024], start=True, stop=True))
        return vop(lambda v: v.tensor_copy(cwrep[:], PC[:]),
                   waits=[("pe", t2)])

    def emit_ffn_half(xbuf, cw_tok, xnT_tok, extra_wdn_wait=(),
                      extra_wup_wait=()):
        cwrep_tok = emit_cwrep(0, cw_tok, state["last_dn_evac"])
        for e in range(NE):
            up_tok = {}
            gelu_tok = {}
            cwm_tok = {}
            wdn_tok = None
            wdn_v = wdn_sb[:].rearrange("p (hc d) -> p hc d", hc=HC)
            for b_ in range(16):
                wdma = []
                if b_ >= 2:
                    wdma.append(("pe", up_tok[(b_ - 2) * 2 + 1]))
                else:
                    if e == 0:
                        wdma += list(extra_wup_wait)
                    if state["wup_last"][b_ % 2]:
                        wdma.append(("pe", state["wup_last"][b_ % 2]))
                wblk = dma(
                    "gpsimd",
                    wup_bf[b_ % 2][:].rearrange("p (dc h) -> p dc h", dc=DC),
                    Wup[e, :, b_ * 256:(b_ + 1) * 256].rearrange(
                        "(dc p) h -> p dc h", p=128),
                    waits=wdma)
                for sub in range(2):
                    hc = b_ * 2 + sub
                    ps_ = PA if hc % 2 == 0 else PB
                    for st in range(2):
                        for dcc in range(DC):
                            w2 = []
                            if dcc == 0 and st == 0:
                                w2 = [("q_pl", wblk), ("act", xnT_tok)]
                                if hc >= 2:
                                    w2.append(("act", gelu_tok[hc - 2]))
                                elif state["gelu_last"][hc % 2]:
                                    w2.append(("act",
                                               state["gelu_last"][hc % 2]))
                            top(lambda t, hc=hc, st=st, dcc=dcc, ps_=ps_,
                                b_=b_, sub=sub: t.matmul(
                                    ps_[:, st * 512:(st + 1) * 512],
                                    wup_bf[b_ % 2][:, dcc * 256 + sub * 128:
                                                   dcc * 256 + sub * 128 + 128],
                                    xnT[:, dcc * HALF + st * 512:
                                        dcc * HALF + (st + 1) * 512],
                                    start=(dcc == 0), stop=(dcc == DC - 1)),
                                waits=w2, inc=(dcc == DC - 1 and st == 1))
                    up_tok[hc] = em.cnt["pe"]
                    gelu_tok[hc] = sop(lambda s, hc=hc, ps_=ps_, e=e: s.activation(
                        h_sb[:, hc * HALF:(hc + 1) * HALF], ps_[:], AF.Gelu,
                        bias=bup_sb[:, hc * NE + e: hc * NE + e + 1]),
                        waits=[("pe", up_tok[hc])])
                    cwm_tok[hc] = vop(lambda v, hc=hc: v.tensor_tensor(
                        h_sb[:, hc * HALF:(hc + 1) * HALF],
                        h_sb[:, hc * HALF:(hc + 1) * HALF], cwrep[:], ALU.mult),
                        waits=[("act", gelu_tok[hc]), ("dve", cwrep_tok)])
                if b_ % 2 == 1:
                    ch = (b_ - 1) // 2
                    wdnw = []
                    if ch == 0:
                        wdnw = list(extra_wdn_wait) if e == 0 else []
                        if state["last_dn_evac"]:
                            wdnw.append(("dve", state["last_dn_evac"]))
                    wdn_tok = dma(
                        "gpsimd", wdn_v[:, ch * 4:(ch + 1) * 4],
                        Wdn[e, ch * 512:(ch + 1) * 512, :].rearrange(
                            "(hc p) d -> p hc d", p=128),
                        waits=wdnw)
            state["wup_last"][0] = up_tok[29]
            state["wup_last"][1] = up_tok[31]
            state["gelu_last"][0] = gelu_tok[30]
            state["gelu_last"][1] = gelu_tok[31]
            dn_evac = {}
            this_cwrep = cwrep_tok
            for ts in range(NT):
                psd = PC if ts % 2 == 0 else PD
                wf = [("dve", cwm_tok[HC - 1]), ("q_pl", wdn_tok)]
                if ts >= 2:
                    wf.append(("dve", dn_evac[ts - 2]))
                elif ts == 0:
                    wf.append(("dve", this_cwrep))
                    if state["last_dn_evac"]:
                        wf.append(("dve", state["last_dn_evac"]))
                elif ts == 1 and state["last_dn_evac_PD"]:
                    wf.append(("dve", state["last_dn_evac_PD"]))
                first = True
                if e == 0:
                    for dh in range(2):
                        top(lambda t, ts=ts, dh=dh, psd=psd: t.matmul(
                            psd[:, dh * 512:(dh + 1) * 512],
                            cwT_bf[:, ts * 128:(ts + 1) * 128],
                            bdn_bf[:, dh * 512:(dh + 1) * 512],
                            start=True, stop=False),
                            waits=wf if dh == 0 else (), inc=False)
                    first = False
                for hc in range(HC):
                    for dh in range(2):
                        st_ = (hc == 0 and e != 0)
                        top(lambda t, ts=ts, hc=hc, dh=dh, psd=psd, st_=st_:
                            t.matmul(
                                psd[:, dh * 512:(dh + 1) * 512],
                                h_sb[:, hc * HALF + ts * 128:
                                     hc * HALF + (ts + 1) * 128],
                                wdn_sb[:, hc * D + dh * 512:
                                       hc * D + (dh + 1) * 512],
                                start=st_, stop=(hc == HC - 1)),
                            waits=(wf if (first and hc == 0 and dh == 0) else ()),
                            inc=(hc == HC - 1 and dh == 1))
                tdn = em.cnt["pe"]
                dn_evac[ts] = vop(lambda v, ts=ts, psd=psd: v.tensor_tensor(
                    xbuf[:, ts * D:(ts + 1) * D], xbuf[:, ts * D:(ts + 1) * D],
                    psd[:], ALU.add), waits=[("pe", tdn)])
                if ts % 2 == 1:
                    state["last_dn_evac_PD"] = dn_evac[ts]
                if ts == NT - 2:
                    state["dn_evac6"] = dn_evac[ts]
                if ts == NT - 1 and e + 1 < NE:
                    cwrep_tok = emit_cwrep(e + 1, cw_tok, state["dn_evac6"])
            state["last_dn_evac"] = max(dn_evac[NT - 1], dn_evac[NT - 2])
        return max(dn_evac[NT - 1], dn_evac[NT - 2])

    xnT_v = xnT[:].rearrange("p (dc t) -> p dc t", dc=DC)
    ln_tok, _ = emit_ln(x_sb, [("q_sp", c_x0)],
                        lambda ts: xnT_v[:, :, ts * 128:(ts + 1) * 128],
                        xnt_A, sq_A)
    cw1 = emit_gate(ln_tok)
    ffn_h0 = emit_ffn_half(x_sb, cw1, ln_tok,
                           extra_wdn_wait=[("dve", setup_done_dve),
                                           ("pe", setup_done_pe)])
    c_sp0 = dma("sync", xdram_half(xscr1, 0), x_view(x_sb),
                waits=[("dve", ffn_h0)])
    c_x1 = dma("sync", x_view(x_sb), xdram_half(xin, 1), waits=[("q_sp", c_sp0)])
    ln_tok, _ = emit_ln(x_sb, [("q_sp", c_x1)],
                        lambda ts: xnT_v[:, :, ts * 128:(ts + 1) * 128],
                        xnt_A, sq_A)
    cw1b = emit_gate(ln_tok)
    ffn_h1 = emit_ffn_half(x_sb, cw1b, ln_tok)

    if stage == "ffn1":
        c_o1 = dma("sync", xdram_half(yout, 1), x_view(x_sb),
                   waits=[("dve", ffn_h1)])
        c_o0 = dma("gpsimd", yout[0:HALF, :], xscr1[0:HALF, :],
                   waits=[("q_sp", c_sp0)])
        em.op("sync", lambda e: e.wait_ge(SEM["q_sp"], em.cnt["q_sp"]))
        em.op("sync", lambda e: e.wait_ge(SEM["q_pl"], em.cnt["q_pl"]))
        return finish(nc, em)

    xn1T_v = xn1T[:].rearrange("p (dc j) -> p dc j", dc=DC)
    ln2b_tok, ln2b_xr = emit_ln(
        x_sb, [("dve", ffn_h1)],
        lambda ts: xn1T_v[:, :, HALF + ts * 128: HALF + (ts + 1) * 128],
        xnt_B, sq_B)
    c_sp1 = dma("sync", xdram_half(xscr1, 1), x_view(x_sb),
                waits=[("dve", ffn_h1)])
    c_xr1 = dma("sync", x_view(x_sb), xdram_half(xscr1, 0),
                waits=[("q_sp", c_sp1), ("dve", ln2b_xr)])
    ln2a_tok, ln2a_xr = emit_ln(
        x_sb, [("q_sp", c_xr1)],
        lambda ts: xn1T_v[:, :, ts * 128:(ts + 1) * 128], xnt_B, sq_B)

    xn1T_own = xn1T[:].rearrange("p (dc j) -> p dc j", dc=DC)[:, :, 0:OWN]
    xn1T_par = xn1T[:].rearrange("p (dc j) -> p dc j", dc=DC)[:, :, OWN:L]

    def ex_write(eng):
        eng.wait_ge(SEM["act"], EXW["act"])
        for arm in eng.switch_core_id(128):
            q = int(arm) % 2
            eng.dma_start(xch[q], xn1T_own).then_inc(SEM["q_sp"], 16)
        eng.wait_ge(SEM["q_sp"], EXW["q_sp_after_data"])
        for arm in eng.switch_core_id(128):
            q = int(arm) % 2
            eng.dma_start(Ffl[q:q + 1, :], flagv_sb[0:1, :]).then_inc(SEM["q_sp"], 16)
        eng.wait_ge(SEM["q_sp"], EXW["q_sp_after_flag"])
        tgt = eng.value_load(nonce_sb[0:1, 1:2])
        with eng.register("spin_t") as rt:
            eng.reg_mov(rt, 0)
            eng.br("spin_lp")
            with nc.body("spin_lp", valid_engines=[eng.engine]):
                eng.reg_alu(rt, rt, 16, op=ALU.add)
                eng.dma_start(flag_rd[0:1, :],
                              Ffl.rearrange("a b -> (a b)")[None, :]
                              ).then_inc(SEM["q_spin"], 16)
                eng.wait_ge(SEM["q_spin"], rt)
                v0 = eng.value_load(flag_rd[0:1, 0:1])
                eng.br_ne(v0, tgt, on_true="spin_lp", on_false="spin_c1")
            with nc.body("spin_c1", valid_engines=[eng.engine]):
                v1 = eng.value_load(flag_rd[0:1, 128:129])
                eng.br_ne(v1, tgt, on_true="spin_lp", on_false="spin_dn")
            nc.switch_bb("spin_dn")
        for arm in eng.switch_core_id(128):
            q = int(arm) % 2
            eng.dma_start(xn1T_par, xch[1 - q]).then_inc(SEM["q_sp"], 16)

    EXW = {"act": max(ln2a_tok, ln2b_tok)}
    EXW["q_sp_after_data"] = em.cnt["q_sp"] + 16
    EXW["q_sp_after_flag"] = em.cnt["q_sp"] + 32
    toks = em.raw("sync", ex_write,
                  bump=[("q_sp", 48), ("q_spin", 0)])
    c_xn1T_full = toks["q_sp"]

    if stage == "xn1":
        c_d = dma("gpsimd",
                  yout.rearrange("(p r) d -> p r d", p=128),
                  xn1T[:, 0:16384].rearrange("p (r d) -> p r d", d=1024),
                  waits=[("q_sp", c_xn1T_full)])
        em.op("sync", lambda e: e.wait_ge(SEM["q_sp"], em.cnt["q_sp"]))
        em.op("gpsimd", lambda e: e.wait_ge(SEM["q_pl"], em.cnt["q_pl"]))
        return finish(nc, em)

    attn_pe0 = em.cnt["pe"]
    dma("sync", qcos[:], qcos_i[:], waits=[("pe", attn_pe0)])
    dma("sync", qsin[:], qsin_i[:])
    dma("sync", kcos[:], kcos_i[:])
    c_qtab = c_ktab = dma("sync", ksin[:], ksin_i[:])
    c_vinit = vop(lambda v: v.memset(v_hp[:], 1.0), waits=[("pe", attn_pe0)])
    c_onesf = vop(lambda v: v.memset(onesf[:], 1.0), waits=[("pe", attn_pe0)])

    wq_tok = {}
    att_state = {"av_prev": [0, 0], "last_exp": [0, 0], "last_avpe": [0, 0],
                 "rep_evac": 0, "last_ca": 0}
    qkv_last_pe = {}

    def emit_qkv(hp):
        bufs = [wqkv_bf[(hp % 2) * 3 + i] for i in range(3)]
        wsrc = [Wq, Wk, Wv]
        dtoks = []
        for i in range(3):
            w = []
            if hp >= 2 and (hp - 2, i) in wq_tok:
                w.append(("pe", wq_tok[(hp - 2, i)]))
            if hp < 2:
                w.append(("pe", attn_pe0))
            dtoks.append(dma(
                "gpsimd",
                bufs[i][:].rearrange("p (dc n) -> p dc n", dc=DC),
                wsrc[i][:, hp * 128:(hp + 1) * 128].rearrange(
                    "(dc p) n -> p dc n", p=128), waits=w))
        hp_guard_dve = ("dve", em.cnt["dve"])
        hp_guard_act = ("act", em.cnt["act"])
        rope_done = []
        for which, nstrip, tabcs, W_i in ((0, 4, (qcos, qsin), 0),
                                          (1, 8, (kcos, ksin), 1)):
            for st in range(nstrip):
                reg = PD[:, 0:512] if st % 2 == 0 else PD[:, 512:1024]
                for dcc in range(DC):
                    w = []
                    if dcc == 0:
                        w = [("q_pl", dtoks[W_i]), ("q_sp", c_xn1T_full),
                             ("q_sp", c_qtab if which == 0 else c_ktab)]
                        if len(rope_done) >= 2:
                            w.append(("act", rope_done[-2][0]))
                        else:
                            w.append(hp_guard_dve)
                    top(lambda t, st=st, dcc=dcc, reg=reg, b_=bufs[W_i]: t.matmul(
                        reg, b_[:, dcc * 128:(dcc + 1) * 128],
                        xn1T[:, dcc * L + st * 512: dcc * L + (st + 1) * 512],
                        start=(dcc == 0), stop=False),
                        waits=w, inc=False)
                bofs = (0 if which == 0 else D) + hp * 128
                top(lambda t, reg=reg, bofs=bofs: t.matmul(
                    reg, qkvp_b[0:1, bofs:bofs + 128], ones[0:1, 0:512],
                    start=False, stop=True),
                    waits=[("q_pl", c_bq), ("q_sp", c_consts)])
                tproj = em.cnt["pe"]
                wce = [("pe", tproj)]
                if att_state["last_ca"]:
                    wce.append(("dve", att_state["last_ca"]))
                ce = sop(lambda s, reg=reg: s.copy(g_sb[:], reg), waits=wce)
                w_rot = [("act", ce), ("q_sp", c_prot)]
                if len(rope_done) >= 1:
                    w_rot.append(("dve", rope_done[-1][1]))
                else:
                    w_rot.append(hp_guard_dve)
                trot = top(lambda t: t.matmul(PC[:, 0:512], prot[:], g_sb[:],
                                              start=True, stop=True),
                           waits=w_rot)
                tc = tabcs[0][:, st * 512:(st + 1) * 512]
                tsn = tabcs[1][:, st * 512:(st + 1) * 512]
                dsts = qT_h if which == 0 else kT_h
                ca = vop(lambda v, tc=tc: v.tensor_tensor(
                    t1_sb[:], g_sb[:], tc, ALU.mult), waits=[("act", ce)])
                att_state["last_ca"] = ca
                cb = vop(lambda v, tsn=tsn: v.tensor_tensor(
                    t2_sb[:], PC[:, 0:512], tsn, ALU.mult), waits=[("pe", trot)])
                lastv = 0
                for hh in range(2):
                    o = hh * 64
                    lastv = vop(lambda v, o=o, hh=hh, st=st, dsts=dsts:
                                v.tensor_add(
                                    dsts[hh][:, st * 512:(st + 1) * 512],
                                    t1_sb[o:o + 64, :], t2_sb[o:o + 64, :]),
                                waits=[("dve", ca), ("dve", cb)])
                rope_done.append((ce, lastv))
        qk_done = rope_done[-1][1]
        v_ev = {}
        for jt in range(32):
            reg = PD[:, 0:512] if jt % 2 == 0 else PD[:, 512:1024]
            for dcc in range(DC):
                w = []
                if dcc == 0:
                    w = [("q_pl", dtoks[2]), ("act", rope_done[-1][0])]
                    if jt >= 2:
                        w.append(("dve", v_ev[jt - 2]))
                top(lambda t, jt=jt, dcc=dcc, reg=reg: t.matmul(
                    reg[:, 0:128], xn1T[:, dcc * L + jt * 128: dcc * L + (jt + 1) * 128],
                    bufs[2][:, dcc * 128:(dcc + 1) * 128],
                    start=(dcc == 0), stop=False), waits=w, inc=False)
            top(lambda t, reg=reg, hp=hp: t.matmul(
                reg[:, 0:128], ones[0:1, 0:128],
                qkvp_b[0:1, 2 * D + hp * 128: 2 * D + hp * 128 + 128],
                start=False, stop=True), waits=[])
            tv = em.cnt["pe"]
            cva = vop(lambda v, jt=jt, reg=reg: v.tensor_copy(
                v_hp[:, jt * 130:jt * 130 + 64], reg[:, 0:64]),
                waits=[("pe", tv), ("dve", c_vinit)])
            v_ev[jt] = vop(lambda v, jt=jt, reg=reg: v.tensor_copy(
                v_hp[:, jt * 130 + 65:jt * 130 + 129], reg[:, 64:128]),
                waits=[("dve", cva)])
        qkv_last_pe[hp] = em.cnt["pe"]
        wq_tok[(hp, 0)] = qkv_last_pe[hp]
        wq_tok[(hp, 1)] = qkv_last_pe[hp]
        wq_tok[(hp, 2)] = qkv_last_pe[hp]
        return qk_done, v_ev[31], rope_done[-1][0]

    def emit_scores_av_pair(hp, qk_tok, v_tok):
        dcc = hp
        pts = [pt_sb, pt_sb2]
        sregs = [PA, PB]
        for ist in range(4):
            avreg = [PC[0:65, 0:512], PC[0:65, 512:1024]]
            exp_tok = {}
            av_tok = {}
            for grp in range(16):
                for hh in range(2):
                    w = []
                    if grp == 0:
                        w = [("dve", qk_tok), ("dve", v_tok)]
                        if att_state["av_prev"][hh]:
                            w.append(("dve", att_state["av_prev"][hh]))
                        if att_state["last_exp"][hh]:
                            w.append(("act", att_state["last_exp"][hh]))
                        if att_state["rep_evac"]:
                            w.append(("act", att_state["rep_evac"]))
                    else:
                        w.append(("act", exp_tok[(grp - 1, hh)]))
                    for sub in range(2):
                        jt = grp * 2 + sub
                        top(lambda t, jt=jt, sreg=sregs[hh], ist=ist, hh=hh,
                            sub=sub: t.matmul(
                                sreg[:, sub * 512:(sub + 1) * 512],
                                kT_h[hh][:, jt * 128:(jt + 1) * 128],
                                qT_h[hh][:, ist * 512:(ist + 1) * 512],
                                start=True, stop=True),
                            waits=w if sub == 0 else (), inc=(sub == 1))
                    ts_tok = em.cnt["pe"]
                    wexp = [("pe", ts_tok), ("dve", c_negc)]
                    if grp >= 2:
                        wexp.append(("pe", av_tok[(grp - 2, hh)]))
                    elif att_state["last_avpe"][hh]:
                        wexp.append(("pe", att_state["last_avpe"][hh]))
                    if grp == 0:
                        wexp.append(("dve", qk_tok))
                    exp_tok[(grp, hh)] = sop(
                        lambda s, sreg=sregs[hh], ptb=pts[hh][grp % 2]:
                        s.activation(ptb[:], sreg[:], AF.Exp, bias=negc,
                                     scale=0.125), waits=wexp)
                for hh in range(2):
                    if grp == 0:
                        continue
                    g2 = grp - 1
                    for sub in range(2):
                        jt = g2 * 2 + sub
                        top(lambda t, jt=jt, sub=sub, hh=hh, g2=g2,
                            ptb=pts[hh][g2 % 2]: t.matmul(
                                avreg[hh],
                                v_hp[:, jt * 130 + hh * 65:
                                     jt * 130 + hh * 65 + 65],
                                ptb[:, sub * 512:(sub + 1) * 512],
                                start=(g2 == 0 and sub == 0), stop=False),
                            waits=[("act", exp_tok[(g2, hh)])]
                            if sub == 0 else (), inc=(sub == 1))
                    av_tok[(g2, hh)] = em.cnt["pe"]
            for hh in range(2):
                g2 = 15
                for sub in range(2):
                    jt = g2 * 2 + sub
                    top(lambda t, jt=jt, sub=sub, hh=hh,
                        ptb=pts[hh][g2 % 2]: t.matmul(
                            avreg[hh],
                            v_hp[:, jt * 130 + hh * 65: jt * 130 + hh * 65 + 65],
                            ptb[:, sub * 512:(sub + 1) * 512],
                            start=False, stop=(sub == 1)),
                        waits=[("act", exp_tok[(g2, hh)])] if sub == 0 else (),
                        inc=(sub == 1))
                av_tok[(15, hh)] = em.cnt["pe"]
            for hh in range(2):
                cr = vop(lambda v, hh=hh: v.reciprocal(
                    recip_sb[:], avreg[hh][64:65, :]),
                    waits=[("pe", av_tok[(15, hh)])])
                wrep = [("dve", cr), ("dve", c_onesf)]
                if att_state["rep_evac"]:
                    wrep.append(("act", att_state["rep_evac"]))
                trep = top(lambda t: t.matmul(PD[0:64, 0:512], onesf[0:1, :],
                                              recip_sb[0:1, :],
                                              start=True, stop=True),
                           waits=wrep)
                att_state["rep_evac"] = sop(
                    lambda s: s.copy(rep_sb[:], PD[0:64, 0:512]),
                    waits=[("pe", trep)])
                att_state["av_prev"][hh] = vop(
                    lambda v, dcc=dcc, hh=hh, ist=ist: v.tensor_tensor(
                        attn_oT[hh * 64:(hh + 1) * 64,
                                dcc * OWN + ist * 512: dcc * OWN + (ist + 1) * 512],
                        avreg[hh][0:64, :], rep_sb[:], ALU.mult),
                    waits=[("act", att_state["rep_evac"])])
                att_state["last_exp"][hh] = exp_tok[(15, hh)]
                att_state["last_avpe"][hh] = av_tok[(15, hh)]
        return att_state["av_prev"]

    attn_done_tok = 0
    for hp in range(8):
        qk_tok, v_tok, _ = emit_qkv(hp)
        emit_scores_av_pair(hp, qk_tok, v_tok)
    attn_done_tok = max(att_state["av_prev"])

    wp_full = sb("wpfull", 128, DC * D, BF16, 146464, "J")
    attn_last_pe = em.cnt["pe"]
    c_wp = dma("gpsimd", wp_full[:].rearrange("p (dc n) -> p dc n", dc=DC),
               Wp.rearrange("(dc p) n -> p dc n", p=128),
               waits=[("pe", attn_last_pe)])
    proj_spills = []
    for halfp in range(2):
        proj_evac = {}
        for ts in range(NT):
            reg = PA if ts % 2 == 0 else PB
            wxr = [("dve", attn_done_tok)]
            if ts >= 2:
                wxr.append(("dve", proj_evac[ts - 2]))
            xstb = xst[ts % 2]
            cxr = dma("sync", xstb[:],
                      xscr1[halfp * HALF + ts * 128:
                            halfp * HALF + (ts + 1) * 128, :],
                      waits=wxr)
            for dh in range(2):
                for dcc in range(DC):
                    w = []
                    if dcc == 0:
                        w = [("q_pl", c_wp), ("dve", attn_done_tok),
                             ("act", max(att_state["last_exp"]))]
                        if ts >= 2:
                            w.append(("dve", proj_evac[ts - 2]))
                    top(lambda t, ts=ts, dh=dh, dcc=dcc, reg=reg, halfp=halfp:
                        t.matmul(
                            reg[:, dh * 512:(dh + 1) * 512],
                            attn_oT[:, dcc * OWN + halfp * HALF + ts * 128:
                                    dcc * OWN + halfp * HALF + (ts + 1) * 128],
                            wp_full[:, dcc * D + dh * 512: dcc * D + (dh + 1) * 512],
                            start=(dcc == 0), stop=False),
                        waits=w, inc=False)
                top(lambda t, reg=reg, dh=dh: t.matmul(
                    reg[:, dh * 512:(dh + 1) * 512], ones[0:1, 0:128],
                    qkvp_b[0:1, 3 * D + dh * 512: 3 * D + (dh + 1) * 512],
                    start=False, stop=True), waits=[])
            tpj = em.cnt["pe"]
            proj_evac[ts] = vop(lambda v, ts=ts, reg=reg, xstb=xstb:
                                v.tensor_tensor(
                                    x2_sb[:, ts * D:(ts + 1) * D], xstb[:],
                                    reg[:], ALU.add),
                                waits=[("pe", tpj), ("q_sp", cxr)])
        proj_spills.append(dma(
            "sync", xdram_half(xscr2, halfp), x_view(x2_sb),
            waits=[("dve", proj_evac[NT - 1]), ("dve", proj_evac[NT - 2])]))
    proj_all_done = proj_spills[-1]

    if stage == "x2":
        c_o0 = dma("gpsimd", yout[:, :], xscr2[:, :], waits=[("q_sp", proj_all_done)])
        em.op("sync", lambda e: e.wait_ge(SEM["q_sp"], em.cnt["q_sp"]))
        em.op("gpsimd", lambda e: e.wait_ge(SEM["q_pl"], em.cnt["q_pl"]))
        return finish(nc, em)

    state["last_dn_evac"] = 0
    state["last_dn_evac_PD"] = 0
    prev_out = 0
    for halfp in range(2):
        wcx = [("q_sp", proj_all_done)]
        if halfp:
            wcx += [("dve", state["last_dn_evac"]), ("q_sp", prev_out)]
        else:
            wcx.append(("pe", em.cnt["pe"]))
        cx = dma("sync", x_view(x_sb), xdram_half(xscr2, halfp), waits=wcx)
        ln_tok3, _ = emit_ln(x_sb, [("q_sp", cx)],
                             lambda ts: xnT_v[:, :, ts * 128:(ts + 1) * 128],
                             xnt_A, sq_A)
        cw3 = emit_gate(ln_tok3)
        extra = ([("pe", em.cnt["pe"])], [("pe", em.cnt["pe"])]) if halfp == 0 else ((), ())
        ffn2 = emit_ffn_half(x_sb, cw3, ln_tok3, extra_wdn_wait=extra[0],
                             extra_wup_wait=extra[1])
        prev_out = dma("sync", xdram_half(yout, halfp), x_view(x_sb),
                       waits=[("dve", ffn2)])

    em.op("sync", lambda e: e.wait_ge(SEM["q_sp"], em.cnt["q_sp"]))
    em.op("gpsimd", lambda e: e.wait_ge(SEM["q_pl"], em.cnt["q_pl"]))
    return finish(nc, em)


SEM = {}


def finish(nc, em):
    global SEM
    SEM = {s: nc.alloc_semaphore("sem_" + s) for s in em.cnt}
    with nc.Block() as blk:
        for eng_name in Em.ENGS:
            evs = em.ev[eng_name]
            if not evs:
                continue

            def make(evs):
                def emit_all(eng):
                    for build, waits, inc in evs:
                        if waits is None:
                            build(eng)
                            continue
                        for s, v in waits:
                            if v:
                                eng.wait_ge(SEM[s], v)
                        ins = build(eng)
                        if inc is not None and ins is not None:
                            ins.then_inc(SEM[inc[0]], inc[1])
                return emit_all

            getattr(blk, eng_name)(make(evs))
    return nc


def _shard_inputs(inputs):
    x = np.ascontiguousarray(np.asarray(inputs["L0_attn_x"], dtype=np.float32))
    consts = {
        "identi": np.eye(128, dtype=np.float32).astype(BF),
        "proti": _prot_matrix(),
        "seli": _sel_matrix(),
        "onesi": np.ones((1, 512), dtype=np.float32).astype(BF),
    }
    nonce = int(np.random.randint(1, 1 << 28))
    nvec = (np.arange(128, dtype=np.int64) + nonce).astype(np.int32)
    shared = {
        "Wq": np.ascontiguousarray(inputs["L0_attn_Wq"], dtype=np.float32),
        "Wk": np.ascontiguousarray(inputs["L0_attn_Wk"], dtype=np.float32),
        "Wv": np.ascontiguousarray(inputs["L0_attn_Wv"], dtype=np.float32),
        "Wp": np.ascontiguousarray(inputs["L0_attn_Wp"], dtype=np.float32),
        "bqkvp": np.concatenate([
            np.asarray(inputs["L0_attn_bq"], dtype=np.float32),
            np.asarray(inputs["L0_attn_bk"], dtype=np.float32),
            np.asarray(inputs["L0_attn_bv"], dtype=np.float32),
            np.asarray(inputs["L0_attn_bp"], dtype=np.float32)])[None, :],
        "Wg": np.ascontiguousarray(inputs["L0_ffn_Wg"], dtype=np.float32),
        "Wup": np.ascontiguousarray(inputs["L0_ffn_Wup"], dtype=np.float32),
        "bupb": np.ascontiguousarray(inputs["L0_ffn_bup"], dtype=np.float32),
        "Wdn": np.ascontiguousarray(inputs["L0_ffn_Wdown"], dtype=np.float32),
        "bdnb": np.ascontiguousarray(inputs["L0_ffn_bdown"], dtype=np.float32),
        "noncei": nvec[None, :],
        "flagvi": np.full((1, 128), nonce + 1, dtype=np.int32),
        **consts,
    }
    in_maps = []
    for c in range(NCORES):
        b, hh = c // 2, c % 2
        own_pos = np.arange(hh * OWN, (hh + 1) * OWN)
        all_pos = np.concatenate([own_pos, np.arange((1 - hh) * OWN,
                                                     (2 - hh) * OWN)])
        m = dict(shared)
        m["xin"] = np.ascontiguousarray(x[b, hh * OWN:(hh + 1) * OWN, :])
        qt = _rope_tab(own_pos)
        kt = _rope_tab(all_pos)
        m["qcosi"] = np.tile(qt[0:64], (2, 1)).astype(BF)
        m["qsini"] = np.tile(qt[64:128], (2, 1)).astype(BF)
        m["kcosi"] = np.tile(kt[0:64], (2, 1)).astype(BF)
        m["ksini"] = np.tile(kt[64:128], (2, 1)).astype(BF)
        in_maps.append(m)
    return in_maps


def kernel(**inputs):
    in_maps = _shard_inputs(inputs)
    nc = build_program("full")
    res = run_bass_kernel_spmd(nc, in_maps, core_ids=list(range(NCORES)))
    out = np.empty((4, L, D), dtype=np.float32)
    for c in range(NCORES):
        b, hh = c // 2, c % 2
        out[b, hh * OWN:(hh + 1) * OWN, :] = res.results[c]["yout"]
    return out


# version 08
# speedup vs baseline: 2.2105x; 2.2105x over previous
import numpy as np
import ml_dtypes
import concourse.bass as bass
import concourse.mybir as mybir
from concourse.bass_utils import run_bass_kernel_spmd

F32 = mybir.dt.float32
BF16 = mybir.dt.bfloat16
I32 = mybir.dt.int32
AF = mybir.ActivationFunctionType
ALU = mybir.AluOpType
AX = mybir.AxisListType

D = 1024
NH = 16
HD = 64
NE = 12
HF = 4096
L = 4096
OWN = 2048
HALF = 1024
NT = 8
DC = 8
HC = 32
NCORES = 8
SHIFT = -14.0
BF = ml_dtypes.bfloat16


class Em:
    ENGS = ("sync", "tensor", "vector", "scalar", "gpsimd")

    def __init__(self):
        self.ev = {k: [] for k in self.ENGS}
        self.cnt = {s: 0 for s in ("q_sp", "q_pl", "q_spin", "pe", "act", "dve", "pl")}

    def op(self, eng, build, waits=(), inc=None):
        w = tuple(waits)
        self.ev[eng].append((build, w, inc))
        if inc is not None:
            self.cnt[inc[0]] += inc[1]
            return self.cnt[inc[0]]
        return None

    def raw(self, eng, fn, bump=()):
        self.ev[eng].append((fn, None, None))
        for s, a in bump:
            self.cnt[s] += a
        return {s: self.cnt[s] for s, _ in bump}


def _sel_matrix():
    m = np.zeros((12, 12 * 128), dtype=np.float32)
    for e in range(12):
        m[e, e * 128:(e + 1) * 128] = 1.0
    return m.astype(BF)


def _prot_matrix():
    m = np.zeros((128, 128), dtype=np.float32)
    for blk in range(2):
        o = blk * 64
        for i in range(32):
            m[o + 2 * i + 1, o + 2 * i] = -1.0
            m[o + 2 * i, o + 2 * i + 1] = 1.0
    return m.astype(BF)


def _rope_tab(positions):
    inv = 1.0 / (10000.0 ** (np.arange(0, HD, 2, dtype=np.float64) / HD))
    fr = positions[None, :].astype(np.float64) * np.repeat(inv, 2)[:, None]
    return np.concatenate([np.cos(fr), np.sin(fr)], axis=0).astype(np.float32)


def build_program(stage="full"):
    nc = bass.Bass("TRN2", target_bir_lowering=False, debug=False)
    em = Em()

    xin = nc.dram_tensor("xin", [OWN, D], F32, kind="ExternalInput").ap()
    Wq = nc.dram_tensor("Wq", [D, D], F32, kind="ExternalInput").ap()
    Wk = nc.dram_tensor("Wk", [D, D], F32, kind="ExternalInput").ap()
    Wv = nc.dram_tensor("Wv", [D, D], F32, kind="ExternalInput").ap()
    Wp = nc.dram_tensor("Wp", [D, D], F32, kind="ExternalInput").ap()
    bqkvp = nc.dram_tensor("bqkvp", [1, 4 * D], F32, kind="ExternalInput").ap()
    Wg = nc.dram_tensor("Wg", [D, NE], F32, kind="ExternalInput").ap()
    Wup = nc.dram_tensor("Wup", [NE, D, HF], F32, kind="ExternalInput").ap()
    bup = nc.dram_tensor("bupb", [NE, HF], F32, kind="ExternalInput").ap()
    Wdn = nc.dram_tensor("Wdn", [NE, HF, D], F32, kind="ExternalInput").ap()
    bdn = nc.dram_tensor("bdnb", [NE, D], F32, kind="ExternalInput").ap()
    ident_i = nc.dram_tensor("identi", [128, 128], BF16, kind="ExternalInput").ap()
    prot_i = nc.dram_tensor("proti", [128, 128], BF16, kind="ExternalInput").ap()
    sel_i = nc.dram_tensor("seli", [12, 12 * 128], BF16, kind="ExternalInput").ap()
    ones_i = nc.dram_tensor("onesi", [1, 512], BF16, kind="ExternalInput").ap()
    qcos_i = nc.dram_tensor("qcosi", [128, OWN], BF16, kind="ExternalInput").ap()
    qsin_i = nc.dram_tensor("qsini", [128, OWN], BF16, kind="ExternalInput").ap()
    kcos_i = nc.dram_tensor("kcosi", [128, L], BF16, kind="ExternalInput").ap()
    ksin_i = nc.dram_tensor("ksini", [128, L], BF16, kind="ExternalInput").ap()
    nonce_i = nc.dram_tensor("noncei", [1, 128], I32, kind="ExternalInput").ap()
    flagv_i = nc.dram_tensor("flagvi", [1, 128], I32, kind="ExternalInput").ap()

    yout = nc.dram_tensor("yout", [OWN, D], F32, kind="ExternalOutput").ap()

    xscr1 = nc.dram_tensor("xscr1", [OWN, D], F32, kind="Internal").ap()
    xscr2 = nc.dram_tensor("xscr2", [OWN, D], F32, kind="Internal").ap()
    xch = nc.dram_tensor("xch", [2, 128, DC, OWN], BF16, kind="Internal",
                         addr_space="Shared").ap()
    Ffl = nc.dram_tensor("fl", [2, 128], I32, kind="Internal",
                         addr_space="Shared").ap()

    SB_BASE = 16544
    SBUF_LIMIT = 229376
    _spans = []

    def sb(name, p, cols, dtype, off, group):
        off = off + SB_BASE
        end = off + cols * mybir.dt.size(dtype)
        assert end <= SBUF_LIMIT, (name, off, end)
        _spans.append((name, off, end, group))
        return nc.alloc_sbuf_tensor_at(name, [p, cols], dtype, offset=off)

    ident = sb("ident", 128, 128, BF16, 0, "P")
    prot = sb("prot", 128, 128, BF16, 256, "P")
    sel = sb("sel", 12, 12 * 128, BF16, 512, "P")
    ones = sb("ones", 1, 512, BF16, 3584, "P")
    qkvp_b = sb("qkvpb", 1, 4 * D, BF16, 4608, "P")
    wg_sb = sb("wg", 128, DC * NE, BF16, 12800, "P")
    bup_sb = sb("bupsb", 128, HC * NE, F32, 12992, "P")
    bdn_bf = sb("bdnbf", 12, D, BF16, 14528, "P")
    cwT_bf = sb("cwT", 12, HALF, BF16, 16576, "P")
    cw_all = sb("cwall", 128, NT * NE, F32, 18624, "P")
    cw_bf = sb("cwbf", 128, NT * NE, BF16, 19008, "P")
    lg_t = sb("lg", 128, NE, F32, 19200, "P")
    msk_t = sb("msk", 128, NE, F32, 19264, "P")
    tmp_t = sb("tmp", 128, NE, F32, 19328, "P")
    l2_t = sb("l2", 128, NE, F32, 19392, "P")
    smalls = sb("smalls", 128, 8, F32, 19456, "P")
    m1_t = smalls[:, 0:1]
    m1n_t = smalls[:, 1:2]
    mm_t = smalls[:, 2:3]
    rs_t = smalls[:, 3:4]
    sq_m = smalls[:, 4:5]
    var_t = smalls[:, 5:6]
    negc = smalls[:, 6:7]
    nonce_sb = sb("noncesb", 1, 8, I32, 19488, "P")
    lnst = sb("lnst", 128, 24, F32, 19520, "P")
    lg2_all = sb("lg2all", 128, 4 * NE, F32, 19616, "P")
    sel_all = sb("selall", 128, 4 * NE, F32, 19808, "P")
    flagv_sb = sb("flagvsb", 1, 128, I32, 20000, "P")
    flag_rd = sb("flagrd", 1, 256, I32, 20512, "P")
    cwrep = sb("cwrep", 128, HALF, BF16, 21536, "P")

    x_sb = sb("x", 128, NT * D, F32, 23584, "F")
    xnT = sb("xnT", 128, DC * HALF, BF16, 56352, "F")
    wup_bf = [sb(f"wup{i}", 128, DC * 256, BF16, 72736 + 4096 * i, "F")
              for i in range(2)]
    h_sb = sb("h", 128, HC * HALF, BF16, 80928, "F")
    wdn_sb = sb("wdn", 128, HC * D, BF16, 146464, "F")

    bup_t = sb("bupt", 12, HF, F32, 146464, "S")
    identf = sb("identf", 128, 128, F32, 162848, "S")

    xnt_A = (sb("xntA0", 128, D, BF16, 80928, "S"),
             sb("xntA1", 128, D, BF16, 87072, "S"))
    sq_A = sb("sqA", 128, D, F32, 82976, "S")
    xnt_B = (sb("xntB0", 128, D, BF16, 121888, "S"),
             sb("xntB1", 128, D, BF16, 123936, "S"))
    sq_B = sb("sqB", 128, D, F32, 125984, "S")

    xn1T = sb("xn1T", 128, DC * L, BF16, 56352, "A")
    qcos = sb("qcos", 128, OWN, BF16, 121888, "A")
    qsin = sb("qsin", 128, OWN, BF16, 125984, "A")
    kcos = sb("kcos", 128, L, BF16, 130080, "A")
    ksin = sb("ksin", 128, L, BF16, 138272, "A")
    kT_h = [sb(f"kT{i}", 64, L, BF16, 146464 + 8192 * i, "A") for i in range(2)]
    qT_h = [sb(f"qT{i}", 64, OWN, BF16, 162848 + 4096 * i, "A") for i in range(2)]
    v_hp = sb("vhp", 128, 32 * 130, BF16, 171040, "A")
    pt_sb = [sb(f"pt{i}", 128, 1024, BF16, 179360 + 2048 * i, "A") for i in range(2)]
    pt_sb2 = [sb(f"ptb{i}", 128, 1024, BF16, 183456 + 2048 * i, "A") for i in range(2)]
    g_sb = sb("gsb", 128, 512, BF16, 183456, "A")
    t1_sb = sb("t1", 128, 512, F32, 184480, "A")
    t2_sb = sb("t2", 128, 512, F32, 186528, "A")
    wqkv_bf = [sb(f"wqkv{i}", 128, DC * 128, BF16, 188576 + 2048 * i, "A")
               for i in range(6)]
    wp_bf = [sb(f"wp{i}", 128, D, BF16, 200864 + 2048 * i, "A") for i in range(2)]
    recip_sb = sb("recip", 1, 512, F32, 204960, "A")
    rep_sb = sb("repsb", 64, 512, F32, 207008, "A")
    onesf = sb("onesf", 1, 64, F32, 209056, "A")
    attn_oT = sb("attnoT", 128, DC * OWN, BF16, 23584, "A")
    x2_sb = sb("x2", 128, NT * D, F32, 56352, "J")
    xst = [sb(f"xst{i}", 128, D, F32, 89120 + 4096 * i, "J") for i in range(2)]

    PA = nc.alloc_psum_tensor("psA", [128, 1024], F32)
    PB = nc.alloc_psum_tensor("psB", [128, 1024], F32)
    PC = nc.alloc_psum_tensor("psC", [128, 1024], F32)
    PD = nc.alloc_psum_tensor("psD", [128, 1024], F32)
    PBv = [PB[:, 0:512].bitcast(BF16), PB[:, 512:1024].bitcast(BF16)]

    def dma(q, dst, src, waits=()):
        sem = "q_sp" if q == "sync" else "q_pl"
        return em.op(q, lambda e, d=dst, s=src: e.dma_start(d, s),
                     waits=waits, inc=(sem, 16))

    def vop(build, waits=()):
        return em.op("vector", build, waits=waits, inc=("dve", 1))

    def sop(build, waits=()):
        return em.op("scalar", build, waits=waits, inc=("act", 1))

    def top(build, waits=(), inc=True):
        return em.op("tensor", build, waits=waits,
                     inc=("pe", 1) if inc else None)

    def x_view(t):
        return t[:].rearrange("p (tt d) -> p tt d", tt=NT)

    def xdram_half(t, half):
        return t[half * HALF:(half + 1) * HALF].rearrange("(tt p) d -> p tt d", p=128)

    c_ident = dma("sync", ident[:], ident_i[:])
    dma("sync", prot[:], prot_i[:])
    c_prot = em.cnt["q_sp"]
    dma("sync", sel[:], sel_i[:])
    dma("sync", ones[:], ones_i[:])
    dma("sync", nonce_sb[:], nonce_i[0:1, 0:8])
    dma("sync", flagv_sb[:], flagv_i[:])
    c_bupt = dma("sync", bup_t[:], bup[:, :])
    c_consts = em.cnt["q_sp"]
    c_wg = dma("gpsimd", wg_sb[:].rearrange("p (dc e) -> p dc e", dc=DC),
               Wg.rearrange("(dc p) e -> p dc e", p=128))
    c_bq = dma("gpsimd", qkvp_b[:], bqkvp[:])
    c_bdn = dma("gpsimd", bdn_bf[:], bdn[:, :])
    c_x0 = dma("sync", x_view(x_sb), xdram_half(xin, 0))
    c_negc = vop(lambda v: v.memset(negc, SHIFT))
    c_idf = vop(lambda v: v.tensor_copy(identf[:], ident[:]),
                waits=[("q_sp", c_ident)])
    bup_ev = {}
    for hc in range(HC):
        reg = PA[:, 0:512] if hc % 2 == 0 else PA[:, 512:1024]
        w = [("q_sp", c_bupt), ("dve", c_idf)]
        if hc >= 2:
            w.append(("dve", bup_ev[hc - 2]))
        tb = top(lambda t, hc=hc, reg=reg: t.transpose(
            reg[:, 0:12], bup_t[0:12, hc * 128:(hc + 1) * 128],
            identf[0:12, 0:12]), waits=w)
        bup_ev[hc] = vop(lambda v, hc=hc, reg=reg: v.tensor_copy(
            bup_sb[:, hc * NE:(hc + 1) * NE], reg[:, 0:12]),
            waits=[("pe", tb)])
    setup_done_dve = em.cnt["dve"]
    setup_done_pe = em.cnt["pe"]

    PAv8 = PA[:, 0:512].bitcast(BF16)
    PBv8 = PB[:, 0:512].bitcast(BF16)
    m_all = lnst[:, 0:8]
    rstd_all = lnst[:, 8:16]
    v_all = lnst[:, 16:24]

    def emit_ln(xbuf, xload_tok, dest_view, xnt_pair, sq_s):
        psum_guard = ("act", em.cnt["act"])
        last = 0
        for ts in range(NT):
            xt = xbuf[:, ts * D:(ts + 1) * D]
            c1 = vop(lambda v, xt=xt, ts=ts: v.reduce_sum(
                lnst[:, ts:ts + 1], xt, axis=AX.X), waits=list(xload_tok))
            csq = vop(lambda v, xt=xt: v.tensor_mul(sq_s[:], xt, xt),
                      waits=[("dve", last)] if last else list(xload_tok))
            last = vop(lambda v, ts=ts: v.reduce_sum(
                lnst[:, 16 + ts:17 + ts], sq_s[:], axis=AX.X),
                waits=[("dve", csq)])
        c4 = vop(lambda v: v.tensor_scalar_mul(m_all, m_all, 1.0 / D),
                 waits=[("dve", last)])
        c5 = vop(lambda v: v.tensor_scalar_mul(v_all, v_all, 1.0 / D),
                 waits=[("dve", c4)])
        c6 = vop(lambda v: v.tensor_mul(rstd_all, m_all, m_all),
                 waits=[("dve", c5)])
        c7 = vop(lambda v: v.tensor_tensor(v_all, v_all, rstd_all, ALU.subtract),
                 waits=[("dve", c6)])
        c8 = vop(lambda v: v.tensor_scalar_add(v_all, v_all, 1e-5),
                 waits=[("dve", c7)])
        c9 = vop(lambda v: v.reciprocal(rstd_all, v_all), waits=[("dve", c8)])
        c10 = sop(lambda s: s.activation(rstd_all, rstd_all, AF.Sqrt),
                  waits=[("dve", c9)])
        tr_ev = {}
        tr_sv = {}
        norm_ev = {}
        last_xread = 0
        for ts in range(NT):
            xt = xbuf[:, ts * D:(ts + 1) * D]
            xnt_s = xnt_pair[ts % 2]
            wn = [("act", c10)]
            if ts >= 2:
                wn.append(("pe", tr_ev[ts - 2]))
            norm_ev[ts] = vop(lambda v, xt=xt, ts=ts, xnt_s=xnt_s: v.tensor_scalar(
                xnt_s[:], xt, lnst[:, ts:ts + 1], lnst[:, 8 + ts:9 + ts],
                ALU.subtract, ALU.mult), waits=wn)
            last_xread = norm_ev[ts]
            bank = PAv8 if ts % 2 == 0 else PBv8
            for dcc in range(DC):
                w = [("dve", norm_ev[ts]), ("q_sp", c_ident)]
                if dcc == 0:
                    w.append(psum_guard if ts < 2 else ("act", tr_sv[ts - 2]))
                top(lambda t, dcc=dcc, bank=bank, xnt_s=xnt_s: t.transpose(
                    bank[:, dcc * 128:(dcc + 1) * 128],
                    xnt_s[:, dcc * 128:(dcc + 1) * 128],
                    ident[:]), waits=w if dcc == 0 else (),
                    inc=(dcc == DC - 1))
            tr_ev[ts] = em.cnt["pe"]
            dv = dest_view(ts)
            tr_sv[ts] = sop(lambda s, dv=dv, bank=bank: s.copy(
                dv, bank[:].rearrange("p (dc n) -> p dc n", dc=DC)),
                waits=[("pe", tr_ev[ts])])
        return tr_sv[NT - 1], last_xread

    def emit_gate(xnT_tok):
        ev = {}
        psum_guard = ("dve", em.cnt["dve"])
        post = {}
        last_exp = 0
        for ts in range(NT):
            reg = PC[:, 0:512] if ts % 2 == 0 else PC[:, 512:1024]
            for dcc in range(DC):
                w = []
                if dcc == 0:
                    w = [("act", xnT_tok), ("q_pl", c_wg)]
                    if ts >= 2:
                        w.append(("dve", ev[ts - 2]))
                    else:
                        w.append(psum_guard)
                top(lambda t, ts=ts, dcc=dcc, reg=reg: t.matmul(
                    reg[:, 0:NE],
                    xnT[:, dcc * HALF + ts * 128: dcc * HALF + (ts + 1) * 128],
                    wg_sb[:, dcc * NE:(dcc + 1) * NE],
                    start=(dcc == 0), stop=(dcc == DC - 1)),
                    waits=w, inc=(dcc == DC - 1))
            tgate = em.cnt["pe"]
            sl = (ts % 4) * NE
            c0 = vop(lambda v, reg=reg: v.tensor_copy(lg_t[:], reg[:, 0:NE]),
                     waits=[("pe", tgate)])
            c1 = vop(lambda v: v.reduce_max(m1_t, lg_t[:], axis=AX.X),
                     waits=[("dve", c0)])
            c2 = vop(lambda v: v.tensor_single_scalar(msk_t[:], lg_t[:], m1_t,
                                                      ALU.is_lt),
                     waits=[("dve", c1)])
            c3 = vop(lambda v: v.tensor_scalar(tmp_t[:], msk_t[:], 1e30, -1e30,
                                               ALU.mult, ALU.add),
                     waits=[("dve", c2)])
            c4 = vop(lambda v: v.tensor_mul(l2_t[:], lg_t[:], msk_t[:]),
                     waits=[("dve", c2)])
            c5 = vop(lambda v: v.tensor_add(l2_t[:], l2_t[:], tmp_t[:]),
                     waits=[("dve", c3), ("dve", c4)])
            c6 = vop(lambda v: v.reduce_max(mm_t, l2_t[:], axis=AX.X),
                     waits=[("dve", c5)])
            c7 = vop(lambda v: v.tensor_single_scalar(msk_t[:], l2_t[:], mm_t,
                                                      ALU.is_lt),
                     waits=[("dve", c6)])
            c8 = vop(lambda v: v.tensor_scalar(tmp_t[:], msk_t[:], 1e30, -1e30,
                                               ALU.mult, ALU.add),
                     waits=[("dve", c7)])
            c9 = vop(lambda v: v.tensor_mul(l2_t[:], l2_t[:], msk_t[:]),
                     waits=[("dve", c7)])
            c10 = vop(lambda v: v.tensor_add(l2_t[:], l2_t[:], tmp_t[:]),
                      waits=[("dve", c8), ("dve", c9)])
            c11 = vop(lambda v: v.reduce_max(mm_t, l2_t[:], axis=AX.X),
                      waits=[("dve", c10)])
            wex = [("act", last_exp)] if last_exp else []
            c12 = vop(lambda v, sl=sl: v.tensor_single_scalar(
                sel_all[:, sl:sl + NE], lg_t[:], mm_t, ALU.is_ge),
                waits=[("dve", c11)] + wex)
            c13 = vop(lambda v, sl=sl: v.tensor_single_scalar(
                lg2_all[:, sl:sl + NE], lg_t[:], m1_t, ALU.subtract),
                waits=[("dve", c1)] + wex)
            ev[ts] = max(c12, c13)
            if ts % 4 == 3:
                ce = sop(lambda s: s.activation(lg2_all[:], lg2_all[:], AF.Exp),
                         waits=[("dve", max(ev[ts - 3], ev[ts - 2],
                                            ev[ts - 1], ev[ts]))])
                last_exp = ce
                for t2 in range(ts - 3, ts + 1):
                    sl2 = (t2 % 4) * NE
                    d1 = vop(lambda v, sl2=sl2: v.tensor_tensor(
                        lg2_all[:, sl2:sl2 + NE], lg2_all[:, sl2:sl2 + NE],
                        sel_all[:, sl2:sl2 + NE], ALU.mult),
                        waits=[("act", ce)])
                    d2 = vop(lambda v, sl2=sl2: v.reduce_sum(
                        mm_t, lg2_all[:, sl2:sl2 + NE], axis=AX.X),
                        waits=[("dve", d1)])
                    d3 = vop(lambda v: v.reciprocal(rs_t, mm_t),
                             waits=[("dve", d2)])
                    d4 = vop(lambda v: v.tensor_scalar_mul(rs_t, rs_t, 0.5),
                             waits=[("dve", d3)])
                    post[t2] = vop(lambda v, t2=t2, sl2=sl2: v.tensor_single_scalar(
                        cw_all[:, t2 * NE:(t2 + 1) * NE],
                        lg2_all[:, sl2:sl2 + NE], rs_t, ALU.mult),
                        waits=[("dve", d4)])
        ccast = vop(lambda v: v.tensor_copy(cw_bf[:], cw_all[:]),
                    waits=[("dve", post[NT - 1])])
        w0 = [("dve", ccast), ("act", xnT_tok)]
        for ts in range(NT):
            top(lambda t, ts=ts: t.transpose(
                PAv8[0:12, ts * 128:(ts + 1) * 128],
                cw_bf[:, ts * NE:(ts + 1) * NE], ident[:]),
                waits=w0 if ts == 0 else (), inc=(ts == NT - 1))
        tcw = em.cnt["pe"]
        return vop(lambda v: v.tensor_copy(cwT_bf[0:12, :], PAv8[0:12, :]),
                   waits=[("pe", tcw)])

    state = {"last_dn_evac": 0, "last_dn_evac_PD": 0, "wup_last": [0, 0],
             "gelu_last": [0, 0], "dn_evac6": 0}

    def emit_cwrep(e, cw_tok, pc_guard):
        w0 = [("dve", cw_tok), ("q_sp", c_consts)]
        if pc_guard:
            w0.append(("dve", pc_guard))
        top(lambda t, e=e: t.matmul(
            PC[:, 0:512], sel[:, e * 128:(e + 1) * 128], cwT_bf[:, 0:512],
            start=True, stop=True), waits=w0, inc=False)
        t2 = top(lambda t, e=e: t.matmul(
            PC[:, 512:1024], sel[:, e * 128:(e + 1) * 128],
            cwT_bf[:, 512:1024], start=True, stop=True))
        return vop(lambda v: v.tensor_copy(cwrep[:], PC[:]),
                   waits=[("pe", t2)])

    def emit_ffn_half(xbuf, cw_tok, xnT_tok, extra_wdn_wait=(),
                      extra_wup_wait=()):
        cwrep_tok = emit_cwrep(0, cw_tok, state["last_dn_evac"])
        for e in range(NE):
            up_tok = {}
            gelu_tok = {}
            cwm_tok = {}
            wdn_tok = None
            wdn_v = wdn_sb[:].rearrange("p (hc d) -> p hc d", hc=HC)
            for b_ in range(16):
                wdma = []
                if b_ >= 2:
                    wdma.append(("pe", up_tok[(b_ - 2) * 2 + 1]))
                else:
                    if e == 0:
                        wdma += list(extra_wup_wait)
                    if state["wup_last"][b_ % 2]:
                        wdma.append(("pe", state["wup_last"][b_ % 2]))
                wblk = dma(
                    "gpsimd",
                    wup_bf[b_ % 2][:].rearrange("p (dc h) -> p dc h", dc=DC),
                    Wup[e, :, b_ * 256:(b_ + 1) * 256].rearrange(
                        "(dc p) h -> p dc h", p=128),
                    waits=wdma)
                for sub in range(2):
                    hc = b_ * 2 + sub
                    ps_ = PA if hc % 2 == 0 else PB
                    for st in range(2):
                        for dcc in range(DC):
                            w2 = []
                            if dcc == 0 and st == 0:
                                w2 = [("q_pl", wblk), ("act", xnT_tok)]
                                if hc >= 2:
                                    w2.append(("act", gelu_tok[hc - 2]))
                                elif state["gelu_last"][hc % 2]:
                                    w2.append(("act",
                                               state["gelu_last"][hc % 2]))
                            top(lambda t, hc=hc, st=st, dcc=dcc, ps_=ps_,
                                b_=b_, sub=sub: t.matmul(
                                    ps_[:, st * 512:(st + 1) * 512],
                                    wup_bf[b_ % 2][:, dcc * 256 + sub * 128:
                                                   dcc * 256 + sub * 128 + 128],
                                    xnT[:, dcc * HALF + st * 512:
                                        dcc * HALF + (st + 1) * 512],
                                    start=(dcc == 0), stop=(dcc == DC - 1)),
                                waits=w2, inc=(dcc == DC - 1 and st == 1))
                    up_tok[hc] = em.cnt["pe"]
                    gelu_tok[hc] = sop(lambda s, hc=hc, ps_=ps_, e=e: s.activation(
                        h_sb[:, hc * HALF:(hc + 1) * HALF], ps_[:], AF.Gelu,
                        bias=bup_sb[:, hc * NE + e: hc * NE + e + 1]),
                        waits=[("pe", up_tok[hc])])
                    cwm_tok[hc] = vop(lambda v, hc=hc: v.tensor_tensor(
                        h_sb[:, hc * HALF:(hc + 1) * HALF],
                        h_sb[:, hc * HALF:(hc + 1) * HALF], cwrep[:], ALU.mult),
                        waits=[("act", gelu_tok[hc]), ("dve", cwrep_tok)])
                if b_ % 2 == 1:
                    ch = (b_ - 1) // 2
                    wdnw = []
                    if ch == 0:
                        wdnw = list(extra_wdn_wait) if e == 0 else []
                        if state["last_dn_evac"]:
                            wdnw.append(("dve", state["last_dn_evac"]))
                    wdn_tok = dma(
                        "gpsimd", wdn_v[:, ch * 4:(ch + 1) * 4],
                        Wdn[e, ch * 512:(ch + 1) * 512, :].rearrange(
                            "(hc p) d -> p hc d", p=128),
                        waits=wdnw)
            state["wup_last"][0] = up_tok[29]
            state["wup_last"][1] = up_tok[31]
            state["gelu_last"][0] = gelu_tok[30]
            state["gelu_last"][1] = gelu_tok[31]
            dn_evac = {}
            this_cwrep = cwrep_tok
            for ts in range(NT):
                psd = PC if ts % 2 == 0 else PD
                wf = [("dve", cwm_tok[HC - 1]), ("q_pl", wdn_tok)]
                if ts >= 2:
                    wf.append(("dve", dn_evac[ts - 2]))
                elif ts == 0:
                    wf.append(("dve", this_cwrep))
                    if state["last_dn_evac"]:
                        wf.append(("dve", state["last_dn_evac"]))
                elif ts == 1 and state["last_dn_evac_PD"]:
                    wf.append(("dve", state["last_dn_evac_PD"]))
                first = True
                if e == 0:
                    for dh in range(2):
                        top(lambda t, ts=ts, dh=dh, psd=psd: t.matmul(
                            psd[:, dh * 512:(dh + 1) * 512],
                            cwT_bf[:, ts * 128:(ts + 1) * 128],
                            bdn_bf[:, dh * 512:(dh + 1) * 512],
                            start=True, stop=False),
                            waits=wf if dh == 0 else (), inc=False)
                    first = False
                for hc in range(HC):
                    for dh in range(2):
                        st_ = (hc == 0 and e != 0)
                        top(lambda t, ts=ts, hc=hc, dh=dh, psd=psd, st_=st_:
                            t.matmul(
                                psd[:, dh * 512:(dh + 1) * 512],
                                h_sb[:, hc * HALF + ts * 128:
                                     hc * HALF + (ts + 1) * 128],
                                wdn_sb[:, hc * D + dh * 512:
                                       hc * D + (dh + 1) * 512],
                                start=st_, stop=(hc == HC - 1)),
                            waits=(wf if (first and hc == 0 and dh == 0) else ()),
                            inc=(hc == HC - 1 and dh == 1))
                tdn = em.cnt["pe"]
                dn_evac[ts] = vop(lambda v, ts=ts, psd=psd: v.tensor_tensor(
                    xbuf[:, ts * D:(ts + 1) * D], xbuf[:, ts * D:(ts + 1) * D],
                    psd[:], ALU.add), waits=[("pe", tdn)])
                if ts % 2 == 1:
                    state["last_dn_evac_PD"] = dn_evac[ts]
                if ts == NT - 2:
                    state["dn_evac6"] = dn_evac[ts]
                if ts == NT - 1 and e + 1 < NE:
                    cwrep_tok = emit_cwrep(e + 1, cw_tok, state["dn_evac6"])
            state["last_dn_evac"] = max(dn_evac[NT - 1], dn_evac[NT - 2])
        return max(dn_evac[NT - 1], dn_evac[NT - 2])

    xnT_v = xnT[:].rearrange("p (dc t) -> p dc t", dc=DC)
    ln_tok, _ = emit_ln(x_sb, [("q_sp", c_x0)],
                        lambda ts: xnT_v[:, :, ts * 128:(ts + 1) * 128],
                        xnt_A, sq_A)
    cw1 = emit_gate(ln_tok)
    ffn_h0 = emit_ffn_half(x_sb, cw1, ln_tok,
                           extra_wdn_wait=[("dve", setup_done_dve),
                                           ("pe", setup_done_pe)])
    c_sp0 = dma("sync", xdram_half(xscr1, 0), x_view(x_sb),
                waits=[("dve", ffn_h0)])
    c_x1 = dma("sync", x_view(x_sb), xdram_half(xin, 1), waits=[("q_sp", c_sp0)])
    ln_tok, _ = emit_ln(x_sb, [("q_sp", c_x1)],
                        lambda ts: xnT_v[:, :, ts * 128:(ts + 1) * 128],
                        xnt_A, sq_A)
    cw1b = emit_gate(ln_tok)
    ffn_h1 = emit_ffn_half(x_sb, cw1b, ln_tok)

    if stage == "ffn1":
        c_o1 = dma("sync", xdram_half(yout, 1), x_view(x_sb),
                   waits=[("dve", ffn_h1)])
        c_o0 = dma("gpsimd", yout[0:HALF, :], xscr1[0:HALF, :],
                   waits=[("q_sp", c_sp0)])
        em.op("sync", lambda e: e.wait_ge(SEM["q_sp"], em.cnt["q_sp"]))
        em.op("sync", lambda e: e.wait_ge(SEM["q_pl"], em.cnt["q_pl"]))
        return finish(nc, em)

    xn1T_v = xn1T[:].rearrange("p (dc j) -> p dc j", dc=DC)
    ln2b_tok, ln2b_xr = emit_ln(
        x_sb, [("dve", ffn_h1)],
        lambda ts: xn1T_v[:, :, HALF + ts * 128: HALF + (ts + 1) * 128],
        xnt_B, sq_B)
    c_sp1 = dma("sync", xdram_half(xscr1, 1), x_view(x_sb),
                waits=[("dve", ffn_h1)])
    c_xr1 = dma("sync", x_view(x_sb), xdram_half(xscr1, 0),
                waits=[("q_sp", c_sp1), ("dve", ln2b_xr)])
    ln2a_tok, ln2a_xr = emit_ln(
        x_sb, [("q_sp", c_xr1)],
        lambda ts: xn1T_v[:, :, ts * 128:(ts + 1) * 128], xnt_B, sq_B)

    xn1T_own = xn1T[:].rearrange("p (dc j) -> p dc j", dc=DC)[:, :, 0:OWN]
    xn1T_par = xn1T[:].rearrange("p (dc j) -> p dc j", dc=DC)[:, :, OWN:L]

    def ex_write(eng):
        eng.wait_ge(SEM["act"], EXW["act"])
        for arm in eng.switch_core_id(128):
            q = int(arm) % 2
            eng.dma_start(xch[q], xn1T_own).then_inc(SEM["q_sp"], 16)
        eng.wait_ge(SEM["q_sp"], EXW["q_sp_after_data"])
        for arm in eng.switch_core_id(128):
            q = int(arm) % 2
            eng.dma_start(Ffl[q:q + 1, :], flagv_sb[0:1, :]).then_inc(SEM["q_sp"], 16)
        eng.wait_ge(SEM["q_sp"], EXW["q_sp_after_flag"])
        tgt = eng.value_load(nonce_sb[0:1, 1:2])
        with eng.register("spin_t") as rt:
            eng.reg_mov(rt, 0)
            eng.br("spin_lp")
            with nc.body("spin_lp", valid_engines=[eng.engine]):
                eng.reg_alu(rt, rt, 16, op=ALU.add)
                eng.dma_start(flag_rd[0:1, :],
                              Ffl.rearrange("a b -> (a b)")[None, :]
                              ).then_inc(SEM["q_spin"], 16)
                eng.wait_ge(SEM["q_spin"], rt)
                v0 = eng.value_load(flag_rd[0:1, 0:1])
                eng.br_ne(v0, tgt, on_true="spin_lp", on_false="spin_c1")
            with nc.body("spin_c1", valid_engines=[eng.engine]):
                v1 = eng.value_load(flag_rd[0:1, 128:129])
                eng.br_ne(v1, tgt, on_true="spin_lp", on_false="spin_dn")
            nc.switch_bb("spin_dn")
        for arm in eng.switch_core_id(128):
            q = int(arm) % 2
            eng.dma_start(xn1T_par, xch[1 - q]).then_inc(SEM["q_sp"], 16)

    EXW = {"act": max(ln2a_tok, ln2b_tok)}
    EXW["q_sp_after_data"] = em.cnt["q_sp"] + 16
    EXW["q_sp_after_flag"] = em.cnt["q_sp"] + 32
    toks = em.raw("sync", ex_write,
                  bump=[("q_sp", 48), ("q_spin", 0)])
    c_xn1T_full = toks["q_sp"]

    if stage == "xn1":
        c_d = dma("gpsimd",
                  yout.rearrange("(p r) d -> p r d", p=128),
                  xn1T[:, 0:16384].rearrange("p (r d) -> p r d", d=1024),
                  waits=[("q_sp", c_xn1T_full)])
        em.op("sync", lambda e: e.wait_ge(SEM["q_sp"], em.cnt["q_sp"]))
        em.op("gpsimd", lambda e: e.wait_ge(SEM["q_pl"], em.cnt["q_pl"]))
        return finish(nc, em)

    attn_pe0 = em.cnt["pe"]
    dma("sync", qcos[:], qcos_i[:], waits=[("pe", attn_pe0)])
    dma("sync", qsin[:], qsin_i[:])
    dma("sync", kcos[:], kcos_i[:])
    c_qtab = c_ktab = dma("sync", ksin[:], ksin_i[:])
    c_vinit = vop(lambda v: v.memset(v_hp[:], 1.0), waits=[("pe", attn_pe0)])
    c_onesf = vop(lambda v: v.memset(onesf[:], 1.0), waits=[("pe", attn_pe0)])

    wq_tok = {}
    att_state = {"av_prev": [0, 0], "last_exp": [0, 0], "last_avpe": [0, 0],
                 "rep_evac": 0, "last_ca": 0, "last_trep": 0}
    qkv_last_pe = {}

    def emit_qkv(hp):
        bufs = [wqkv_bf[(hp % 2) * 3 + i] for i in range(3)]
        wsrc = [Wq, Wk, Wv]
        dtoks = []
        for i in range(3):
            w = []
            if hp >= 2 and (hp - 2, i) in wq_tok:
                w.append(("pe", wq_tok[(hp - 2, i)]))
            if hp < 2:
                w.append(("pe", attn_pe0))
            dtoks.append(dma(
                "gpsimd",
                bufs[i][:].rearrange("p (dc n) -> p dc n", dc=DC),
                wsrc[i][:, hp * 128:(hp + 1) * 128].rearrange(
                    "(dc p) n -> p dc n", p=128), waits=w))
        hp_guard_dve = ("dve", em.cnt["dve"])
        hp_guard_act = ("act", em.cnt["act"])
        rope_done = []
        for which, nstrip, tabcs, W_i in ((0, 4, (qcos, qsin), 0),
                                          (1, 8, (kcos, ksin), 1)):
            for st in range(nstrip):
                reg = PD[:, 0:512] if st % 2 == 0 else PD[:, 512:1024]
                for dcc in range(DC):
                    w = []
                    if dcc == 0:
                        w = [("q_pl", dtoks[W_i]), ("q_sp", c_xn1T_full),
                             ("q_sp", c_qtab if which == 0 else c_ktab)]
                        if len(rope_done) >= 2:
                            w.append(("act", rope_done[-2][0]))
                        else:
                            w.append(hp_guard_dve)
                    top(lambda t, st=st, dcc=dcc, reg=reg, b_=bufs[W_i]: t.matmul(
                        reg, b_[:, dcc * 128:(dcc + 1) * 128],
                        xn1T[:, dcc * L + st * 512: dcc * L + (st + 1) * 512],
                        start=(dcc == 0), stop=False),
                        waits=w, inc=False)
                bofs = (0 if which == 0 else D) + hp * 128
                top(lambda t, reg=reg, bofs=bofs: t.matmul(
                    reg, qkvp_b[0:1, bofs:bofs + 128], ones[0:1, 0:512],
                    start=False, stop=True),
                    waits=[("q_pl", c_bq), ("q_sp", c_consts)])
                tproj = em.cnt["pe"]
                wce = [("pe", tproj)]
                if att_state["last_ca"]:
                    wce.append(("dve", att_state["last_ca"]))
                ce = sop(lambda s, reg=reg: s.copy(g_sb[:], reg), waits=wce)
                w_rot = [("act", ce), ("q_sp", c_prot)]
                if len(rope_done) >= 1:
                    w_rot.append(("dve", rope_done[-1][1]))
                else:
                    w_rot.append(hp_guard_dve)
                trot = top(lambda t: t.matmul(PC[:, 0:512], prot[:], g_sb[:],
                                              start=True, stop=True),
                           waits=w_rot)
                tc = tabcs[0][:, st * 512:(st + 1) * 512]
                tsn = tabcs[1][:, st * 512:(st + 1) * 512]
                dsts = qT_h if which == 0 else kT_h
                ca = vop(lambda v, tc=tc: v.tensor_tensor(
                    t1_sb[:], g_sb[:], tc, ALU.mult), waits=[("act", ce)])
                att_state["last_ca"] = ca
                cb = vop(lambda v, tsn=tsn: v.tensor_tensor(
                    t2_sb[:], PC[:, 0:512], tsn, ALU.mult), waits=[("pe", trot)])
                lastv = 0
                for hh in range(2):
                    o = hh * 64
                    lastv = vop(lambda v, o=o, hh=hh, st=st, dsts=dsts:
                                v.tensor_add(
                                    dsts[hh][:, st * 512:(st + 1) * 512],
                                    t1_sb[o:o + 64, :], t2_sb[o:o + 64, :]),
                                waits=[("dve", ca), ("dve", cb)])
                rope_done.append((ce, lastv))
        qk_done = rope_done[-1][1]
        v_ev = {}
        for jt in range(32):
            reg = PD[:, 0:512] if jt % 2 == 0 else PD[:, 512:1024]
            for dcc in range(DC):
                w = []
                if dcc == 0:
                    w = [("q_pl", dtoks[2]), ("act", rope_done[-1][0])]
                    if jt >= 2:
                        w.append(("dve", v_ev[jt - 2]))
                top(lambda t, jt=jt, dcc=dcc, reg=reg: t.matmul(
                    reg[:, 0:128], xn1T[:, dcc * L + jt * 128: dcc * L + (jt + 1) * 128],
                    bufs[2][:, dcc * 128:(dcc + 1) * 128],
                    start=(dcc == 0), stop=False), waits=w, inc=False)
            top(lambda t, reg=reg, hp=hp: t.matmul(
                reg[:, 0:128], ones[0:1, 0:128],
                qkvp_b[0:1, 2 * D + hp * 128: 2 * D + hp * 128 + 128],
                start=False, stop=True), waits=[])
            tv = em.cnt["pe"]
            cva = vop(lambda v, jt=jt, reg=reg: v.tensor_copy(
                v_hp[:, jt * 130:jt * 130 + 64], reg[:, 0:64]),
                waits=[("pe", tv), ("dve", c_vinit)])
            v_ev[jt] = vop(lambda v, jt=jt, reg=reg: v.tensor_copy(
                v_hp[:, jt * 130 + 65:jt * 130 + 129], reg[:, 64:128]),
                waits=[("dve", cva)])
        qkv_last_pe[hp] = em.cnt["pe"]
        wq_tok[(hp, 0)] = qkv_last_pe[hp]
        wq_tok[(hp, 1)] = qkv_last_pe[hp]
        wq_tok[(hp, 2)] = qkv_last_pe[hp]
        return qk_done, v_ev[31], rope_done[-1][0]

    def emit_scores_av_pair(hp, qk_tok, v_tok):
        dcc = hp
        pts = [pt_sb, pt_sb2]
        sregs = [PA, PB]
        for ist in range(4):
            avreg = [PC[0:65, 0:512], PC[0:65, 512:1024]]
            exp_tok = {}
            av_tok = {}
            for grp in range(16):
                for hh in range(2):
                    w = []
                    if grp == 0:
                        w = [("dve", qk_tok), ("dve", v_tok)]
                        if att_state["av_prev"][hh]:
                            w.append(("dve", att_state["av_prev"][hh]))
                        if att_state["last_exp"][hh]:
                            w.append(("act", att_state["last_exp"][hh]))
                        if att_state["rep_evac"]:
                            w.append(("act", att_state["rep_evac"]))
                    else:
                        w.append(("act", exp_tok[(grp - 1, hh)]))
                    for sub in range(2):
                        jt = grp * 2 + sub
                        top(lambda t, jt=jt, sreg=sregs[hh], ist=ist, hh=hh,
                            sub=sub: t.matmul(
                                sreg[:, sub * 512:(sub + 1) * 512],
                                kT_h[hh][:, jt * 128:(jt + 1) * 128],
                                qT_h[hh][:, ist * 512:(ist + 1) * 512],
                                start=True, stop=True),
                            waits=w if sub == 0 else (), inc=(sub == 1))
                    ts_tok = em.cnt["pe"]
                    wexp = [("pe", ts_tok), ("dve", c_negc)]
                    if grp >= 2:
                        wexp.append(("pe", av_tok[(grp - 2, hh)]))
                    elif att_state["last_avpe"][hh]:
                        wexp.append(("pe", att_state["last_avpe"][hh]))
                    if grp == 0:
                        wexp.append(("dve", qk_tok))
                    exp_tok[(grp, hh)] = sop(
                        lambda s, sreg=sregs[hh], ptb=pts[hh][grp % 2]:
                        s.activation(ptb[:], sreg[:], AF.Exp, bias=negc,
                                     scale=0.125), waits=wexp)
                for hh in range(2):
                    if grp == 0:
                        continue
                    g2 = grp - 1
                    for sub in range(2):
                        jt = g2 * 2 + sub
                        top(lambda t, jt=jt, sub=sub, hh=hh, g2=g2,
                            ptb=pts[hh][g2 % 2]: t.matmul(
                                avreg[hh],
                                v_hp[:, jt * 130 + hh * 65:
                                     jt * 130 + hh * 65 + 65],
                                ptb[:, sub * 512:(sub + 1) * 512],
                                start=(g2 == 0 and sub == 0), stop=False),
                            waits=[("act", exp_tok[(g2, hh)])]
                            if sub == 0 else (), inc=(sub == 1))
                    av_tok[(g2, hh)] = em.cnt["pe"]
            for hh in range(2):
                g2 = 15
                for sub in range(2):
                    jt = g2 * 2 + sub
                    top(lambda t, jt=jt, sub=sub, hh=hh,
                        ptb=pts[hh][g2 % 2]: t.matmul(
                            avreg[hh],
                            v_hp[:, jt * 130 + hh * 65: jt * 130 + hh * 65 + 65],
                            ptb[:, sub * 512:(sub + 1) * 512],
                            start=False, stop=(sub == 1)),
                        waits=[("act", exp_tok[(g2, hh)])] if sub == 0 else (),
                        inc=(sub == 1))
                av_tok[(15, hh)] = em.cnt["pe"]
            for hh in range(2):
                wcr = [("pe", av_tok[(15, hh)])]
                if att_state["last_trep"]:
                    wcr.append(("pe", att_state["last_trep"]))
                cr = vop(lambda v, hh=hh: v.reciprocal(
                    recip_sb[:], avreg[hh][64:65, :]), waits=wcr)
                wrep = [("dve", cr), ("dve", c_onesf)]
                if att_state["rep_evac"]:
                    wrep.append(("act", att_state["rep_evac"]))
                trep = top(lambda t: t.matmul(PD[0:64, 0:512], onesf[0:1, :],
                                              recip_sb[0:1, :],
                                              start=True, stop=True),
                           waits=wrep)
                att_state["last_trep"] = trep
                att_state["rep_evac"] = sop(
                    lambda s: s.copy(rep_sb[:], PD[0:64, 0:512]),
                    waits=[("pe", trep)])
                att_state["av_prev"][hh] = vop(
                    lambda v, dcc=dcc, hh=hh, ist=ist: v.tensor_tensor(
                        attn_oT[hh * 64:(hh + 1) * 64,
                                dcc * OWN + ist * 512: dcc * OWN + (ist + 1) * 512],
                        avreg[hh][0:64, :], rep_sb[:], ALU.mult),
                    waits=[("act", att_state["rep_evac"])])
                att_state["last_exp"][hh] = exp_tok[(15, hh)]
                att_state["last_avpe"][hh] = av_tok[(15, hh)]
        return att_state["av_prev"]

    attn_done_tok = 0
    for hp in range(8):
        qk_tok, v_tok, _ = emit_qkv(hp)
        emit_scores_av_pair(hp, qk_tok, v_tok)
    attn_done_tok = max(att_state["av_prev"])

    wp_full = sb("wpfull", 128, DC * D, BF16, 146464, "J")
    attn_last_pe = em.cnt["pe"]
    c_wp = dma("gpsimd", wp_full[:].rearrange("p (dc n) -> p dc n", dc=DC),
               Wp.rearrange("(dc p) n -> p dc n", p=128),
               waits=[("pe", attn_last_pe)])
    proj_spills = []
    for halfp in range(2):
        proj_evac = {}
        for ts in range(NT):
            reg = PA if ts % 2 == 0 else PB
            wxr = [("dve", attn_done_tok)]
            if ts >= 2:
                wxr.append(("dve", proj_evac[ts - 2]))
            xstb = xst[ts % 2]
            cxr = dma("sync", xstb[:],
                      xscr1[halfp * HALF + ts * 128:
                            halfp * HALF + (ts + 1) * 128, :],
                      waits=wxr)
            for dh in range(2):
                for dcc in range(DC):
                    w = []
                    if dcc == 0:
                        w = [("q_pl", c_wp), ("dve", attn_done_tok),
                             ("act", max(att_state["last_exp"]))]
                        if ts >= 2:
                            w.append(("dve", proj_evac[ts - 2]))
                    top(lambda t, ts=ts, dh=dh, dcc=dcc, reg=reg, halfp=halfp:
                        t.matmul(
                            reg[:, dh * 512:(dh + 1) * 512],
                            attn_oT[:, dcc * OWN + halfp * HALF + ts * 128:
                                    dcc * OWN + halfp * HALF + (ts + 1) * 128],
                            wp_full[:, dcc * D + dh * 512: dcc * D + (dh + 1) * 512],
                            start=(dcc == 0), stop=False),
                        waits=w, inc=False)
                top(lambda t, reg=reg, dh=dh: t.matmul(
                    reg[:, dh * 512:(dh + 1) * 512], ones[0:1, 0:128],
                    qkvp_b[0:1, 3 * D + dh * 512: 3 * D + (dh + 1) * 512],
                    start=False, stop=True), waits=[])
            tpj = em.cnt["pe"]
            proj_evac[ts] = vop(lambda v, ts=ts, reg=reg, xstb=xstb:
                                v.tensor_tensor(
                                    x2_sb[:, ts * D:(ts + 1) * D], xstb[:],
                                    reg[:], ALU.add),
                                waits=[("pe", tpj), ("q_sp", cxr)])
        proj_spills.append(dma(
            "sync", xdram_half(xscr2, halfp), x_view(x2_sb),
            waits=[("dve", proj_evac[NT - 1]), ("dve", proj_evac[NT - 2])]))
    proj_all_done = proj_spills[-1]

    if stage == "x2":
        c_o0 = dma("gpsimd", yout[:, :], xscr2[:, :], waits=[("q_sp", proj_all_done)])
        em.op("sync", lambda e: e.wait_ge(SEM["q_sp"], em.cnt["q_sp"]))
        em.op("gpsimd", lambda e: e.wait_ge(SEM["q_pl"], em.cnt["q_pl"]))
        return finish(nc, em)

    state["last_dn_evac"] = 0
    state["last_dn_evac_PD"] = 0
    prev_out = 0
    for halfp in range(2):
        wcx = [("q_sp", proj_all_done)]
        if halfp:
            wcx += [("dve", state["last_dn_evac"]), ("q_sp", prev_out)]
        else:
            wcx.append(("pe", em.cnt["pe"]))
        cx = dma("sync", x_view(x_sb), xdram_half(xscr2, halfp), waits=wcx)
        ln_tok3, _ = emit_ln(x_sb, [("q_sp", cx)],
                             lambda ts: xnT_v[:, :, ts * 128:(ts + 1) * 128],
                             xnt_A, sq_A)
        cw3 = emit_gate(ln_tok3)
        extra = ([("pe", em.cnt["pe"])], [("pe", em.cnt["pe"])]) if halfp == 0 else ((), ())
        ffn2 = emit_ffn_half(x_sb, cw3, ln_tok3, extra_wdn_wait=extra[0],
                             extra_wup_wait=extra[1])
        prev_out = dma("sync", xdram_half(yout, halfp), x_view(x_sb),
                       waits=[("dve", ffn2)])

    em.op("sync", lambda e: e.wait_ge(SEM["q_sp"], em.cnt["q_sp"]))
    em.op("gpsimd", lambda e: e.wait_ge(SEM["q_pl"], em.cnt["q_pl"]))
    return finish(nc, em)


SEM = {}


def finish(nc, em):
    global SEM
    SEM = {s: nc.alloc_semaphore("sem_" + s) for s in em.cnt}
    with nc.Block() as blk:
        for eng_name in Em.ENGS:
            evs = em.ev[eng_name]
            if not evs:
                continue

            def make(evs):
                def emit_all(eng):
                    for build, waits, inc in evs:
                        if waits is None:
                            build(eng)
                            continue
                        for s, v in waits:
                            if v:
                                eng.wait_ge(SEM[s], v)
                        ins = build(eng)
                        if inc is not None and ins is not None:
                            ins.then_inc(SEM[inc[0]], inc[1])
                return emit_all

            getattr(blk, eng_name)(make(evs))
    return nc


def _shard_inputs(inputs):
    x = np.ascontiguousarray(np.asarray(inputs["L0_attn_x"], dtype=np.float32))
    consts = {
        "identi": np.eye(128, dtype=np.float32).astype(BF),
        "proti": _prot_matrix(),
        "seli": _sel_matrix(),
        "onesi": np.ones((1, 512), dtype=np.float32).astype(BF),
    }
    nonce = int(np.random.randint(1, 1 << 28))
    nvec = (np.arange(128, dtype=np.int64) + nonce).astype(np.int32)
    shared = {
        "Wq": np.ascontiguousarray(inputs["L0_attn_Wq"], dtype=np.float32),
        "Wk": np.ascontiguousarray(inputs["L0_attn_Wk"], dtype=np.float32),
        "Wv": np.ascontiguousarray(inputs["L0_attn_Wv"], dtype=np.float32),
        "Wp": np.ascontiguousarray(inputs["L0_attn_Wp"], dtype=np.float32),
        "bqkvp": np.concatenate([
            np.asarray(inputs["L0_attn_bq"], dtype=np.float32),
            np.asarray(inputs["L0_attn_bk"], dtype=np.float32),
            np.asarray(inputs["L0_attn_bv"], dtype=np.float32),
            np.asarray(inputs["L0_attn_bp"], dtype=np.float32)])[None, :],
        "Wg": np.ascontiguousarray(inputs["L0_ffn_Wg"], dtype=np.float32),
        "Wup": np.ascontiguousarray(inputs["L0_ffn_Wup"], dtype=np.float32),
        "bupb": np.ascontiguousarray(inputs["L0_ffn_bup"], dtype=np.float32),
        "Wdn": np.ascontiguousarray(inputs["L0_ffn_Wdown"], dtype=np.float32),
        "bdnb": np.ascontiguousarray(inputs["L0_ffn_bdown"], dtype=np.float32),
        "noncei": nvec[None, :],
        "flagvi": np.full((1, 128), nonce + 1, dtype=np.int32),
        **consts,
    }
    in_maps = []
    for c in range(NCORES):
        b, hh = c // 2, c % 2
        own_pos = np.arange(hh * OWN, (hh + 1) * OWN)
        all_pos = np.concatenate([own_pos, np.arange((1 - hh) * OWN,
                                                     (2 - hh) * OWN)])
        m = dict(shared)
        m["xin"] = np.ascontiguousarray(x[b, hh * OWN:(hh + 1) * OWN, :])
        qt = _rope_tab(own_pos)
        kt = _rope_tab(all_pos)
        m["qcosi"] = np.tile(qt[0:64], (2, 1)).astype(BF)
        m["qsini"] = np.tile(qt[64:128], (2, 1)).astype(BF)
        m["kcosi"] = np.tile(kt[0:64], (2, 1)).astype(BF)
        m["ksini"] = np.tile(kt[64:128], (2, 1)).astype(BF)
        in_maps.append(m)
    return in_maps


def kernel(**inputs):
    in_maps = _shard_inputs(inputs)
    nc = build_program("full")
    res = run_bass_kernel_spmd(nc, in_maps, core_ids=list(range(NCORES)))
    out = np.empty((4, L, D), dtype=np.float32)
    for c in range(NCORES):
        b, hh = c // 2, c % 2
        out[b, hh * OWN:(hh + 1) * OWN, :] = res.results[c]["yout"]
    return out


# version 09 (final)
# speedup vs baseline: 2.3535x; 1.0647x over previous
import numpy as np
import ml_dtypes
import concourse.bass as bass
import concourse.mybir as mybir
from concourse.bass_utils import run_bass_kernel_spmd

F32 = mybir.dt.float32
BF16 = mybir.dt.bfloat16
I32 = mybir.dt.int32
AF = mybir.ActivationFunctionType
ALU = mybir.AluOpType
AX = mybir.AxisListType

D = 1024
NH = 16
HD = 64
NE = 12
HF = 4096
L = 4096
OWN = 2048
HALF = 1024
NT = 8
DC = 8
HC = 32
NCORES = 8
SHIFT = -14.0
BF = ml_dtypes.bfloat16


class Em:
    ENGS = ("sync", "tensor", "vector", "scalar", "gpsimd")

    def __init__(self):
        self.ev = {k: [] for k in self.ENGS}
        self.cnt = {s: 0 for s in ("q_sp", "q_pl", "q_spin", "pe", "act", "dve", "pl")}

    def op(self, eng, build, waits=(), inc=None):
        w = tuple(waits)
        self.ev[eng].append((build, w, inc))
        if inc is not None:
            self.cnt[inc[0]] += inc[1]
            return self.cnt[inc[0]]
        return None

    def raw(self, eng, fn, bump=()):
        self.ev[eng].append((fn, None, None))
        for s, a in bump:
            self.cnt[s] += a
        return {s: self.cnt[s] for s, _ in bump}


def _sel_matrix():
    m = np.zeros((12, 12 * 128), dtype=np.float32)
    for e in range(12):
        m[e, e * 128:(e + 1) * 128] = 1.0
    return m.astype(BF)


def _prot_matrix():
    m = np.zeros((128, 128), dtype=np.float32)
    for blk in range(2):
        o = blk * 64
        for i in range(32):
            m[o + 2 * i + 1, o + 2 * i] = -1.0
            m[o + 2 * i, o + 2 * i + 1] = 1.0
    return m.astype(BF)


def _rope_tab(positions):
    inv = 1.0 / (10000.0 ** (np.arange(0, HD, 2, dtype=np.float64) / HD))
    fr = positions[None, :].astype(np.float64) * np.repeat(inv, 2)[:, None]
    return np.concatenate([np.cos(fr), np.sin(fr)], axis=0).astype(np.float32)


def build_program(stage="full"):
    nc = bass.Bass("TRN2", target_bir_lowering=False, debug=False)
    em = Em()

    xin = nc.dram_tensor("xin", [OWN, D], F32, kind="ExternalInput").ap()
    Wq = nc.dram_tensor("Wq", [D, D], F32, kind="ExternalInput").ap()
    Wk = nc.dram_tensor("Wk", [D, D], F32, kind="ExternalInput").ap()
    Wv = nc.dram_tensor("Wv", [D, D], F32, kind="ExternalInput").ap()
    Wp = nc.dram_tensor("Wp", [D, D], F32, kind="ExternalInput").ap()
    bqkvp = nc.dram_tensor("bqkvp", [1, 4 * D], F32, kind="ExternalInput").ap()
    Wg = nc.dram_tensor("Wg", [D, NE], F32, kind="ExternalInput").ap()
    Wup = nc.dram_tensor("Wup", [NE, D, HF], F32, kind="ExternalInput").ap()
    bup = nc.dram_tensor("bupb", [NE, HF], F32, kind="ExternalInput").ap()
    Wdn = nc.dram_tensor("Wdn", [NE, HF, D], F32, kind="ExternalInput").ap()
    bdn = nc.dram_tensor("bdnb", [NE, D], F32, kind="ExternalInput").ap()
    ident_i = nc.dram_tensor("identi", [128, 128], BF16, kind="ExternalInput").ap()
    prot_i = nc.dram_tensor("proti", [128, 128], BF16, kind="ExternalInput").ap()
    sel_i = nc.dram_tensor("seli", [12, 12 * 128], BF16, kind="ExternalInput").ap()
    ones_i = nc.dram_tensor("onesi", [1, 512], BF16, kind="ExternalInput").ap()
    qcos_i = nc.dram_tensor("qcosi", [128, OWN], BF16, kind="ExternalInput").ap()
    qsin_i = nc.dram_tensor("qsini", [128, OWN], BF16, kind="ExternalInput").ap()
    kcos_i = nc.dram_tensor("kcosi", [128, L], BF16, kind="ExternalInput").ap()
    ksin_i = nc.dram_tensor("ksini", [128, L], BF16, kind="ExternalInput").ap()
    nonce_i = nc.dram_tensor("noncei", [1, 128], I32, kind="ExternalInput").ap()
    flagv_i = nc.dram_tensor("flagvi", [1, 128], I32, kind="ExternalInput").ap()

    yout = nc.dram_tensor("yout", [OWN, D], F32, kind="ExternalOutput").ap()

    xscr1 = nc.dram_tensor("xscr1", [OWN, D], F32, kind="Internal").ap()
    xscr2 = nc.dram_tensor("xscr2", [OWN, D], F32, kind="Internal").ap()
    xch = nc.dram_tensor("xch", [2, 128, DC, OWN], BF16, kind="Internal",
                         addr_space="Shared").ap()
    Ffl = nc.dram_tensor("fl", [2, 128], I32, kind="Internal",
                         addr_space="Shared").ap()

    SB_BASE = 16544
    SBUF_LIMIT = 229376
    _spans = []

    def sb(name, p, cols, dtype, off, group):
        off = off + SB_BASE
        end = off + cols * mybir.dt.size(dtype)
        assert end <= SBUF_LIMIT, (name, off, end)
        _spans.append((name, off, end, group))
        return nc.alloc_sbuf_tensor_at(name, [p, cols], dtype, offset=off)

    ident = sb("ident", 128, 128, BF16, 0, "P")
    prot = sb("prot", 128, 128, BF16, 256, "P")
    sel = sb("sel", 12, 12 * 128, BF16, 512, "P")
    ones = sb("ones", 1, 512, BF16, 3584, "P")
    qkvp_b = sb("qkvpb", 1, 4 * D, BF16, 4608, "P")
    wg_sb = sb("wg", 128, DC * NE, BF16, 12800, "P")
    bup_sb = sb("bupsb", 128, HC * NE, F32, 12992, "P")
    bdn_bf = sb("bdnbf", 12, D, BF16, 14528, "P")
    cwT_bf = sb("cwT", 12, HALF, BF16, 16576, "P")
    cw_all = sb("cwall", 128, NT * NE, F32, 18624, "P")
    cw_bf = sb("cwbf", 128, NT * NE, BF16, 19008, "P")
    lg_t = sb("lg", 128, NE, F32, 19200, "P")
    msk_t = sb("msk", 128, NE, F32, 19264, "P")
    tmp_t = sb("tmp", 128, NE, F32, 19328, "P")
    l2_t = sb("l2", 128, NE, F32, 19392, "P")
    smalls = sb("smalls", 128, 8, F32, 19456, "P")
    m1_t = smalls[:, 0:1]
    m1n_t = smalls[:, 1:2]
    mm_t = smalls[:, 2:3]
    rs_t = smalls[:, 3:4]
    sq_m = smalls[:, 4:5]
    var_t = smalls[:, 5:6]
    negc = smalls[:, 6:7]
    nonce_sb = sb("noncesb", 1, 8, I32, 19488, "P")
    lnst = sb("lnst", 128, 24, F32, 19520, "P")
    lg2_all = sb("lg2all", 128, 4 * NE, F32, 19616, "P")
    sel_all = sb("selall", 128, 4 * NE, F32, 19808, "P")
    flagv_sb = sb("flagvsb", 1, 128, I32, 20000, "P")
    flag_rd = sb("flagrd", 1, 256, I32, 20512, "P")
    cwrep = sb("cwrep", 128, HALF, BF16, 21536, "P")

    x_sb = sb("x", 128, NT * D, F32, 23584, "F")
    xnT = sb("xnT", 128, DC * HALF, BF16, 56352, "F")
    wup_bf = [sb(f"wup{i}", 128, DC * 256, BF16, 72736 + 4096 * i, "F")
              for i in range(2)]
    h_sb = sb("h", 128, HC * HALF, BF16, 80928, "F")
    wdn_sb = sb("wdn", 128, HC * D, BF16, 146464, "F")

    bup_t = sb("bupt", 12, HF, F32, 146464, "S")
    identf = sb("identf", 128, 128, F32, 162848, "S")

    xnt_A = (sb("xntA0", 128, D, BF16, 80928, "S"),
             sb("xntA1", 128, D, BF16, 87072, "S"))
    sq_A = sb("sqA", 128, D, F32, 82976, "S")
    xnt_B = (sb("xntB0", 128, D, BF16, 121888, "S"),
             sb("xntB1", 128, D, BF16, 123936, "S"))
    sq_B = sb("sqB", 128, D, F32, 125984, "S")

    xn1T = sb("xn1T", 128, DC * L, BF16, 56352, "A")
    qcos = sb("qcos", 128, OWN, BF16, 121888, "A")
    qsin = sb("qsin", 128, OWN, BF16, 125984, "A")
    kcos = sb("kcos", 128, L, BF16, 130080, "A")
    ksin = sb("ksin", 128, L, BF16, 138272, "A")
    kT_h = [sb(f"kT{i}", 64, L, BF16, 146464 + 8192 * i, "A") for i in range(2)]
    qT_h = [sb(f"qT{i}", 64, OWN, BF16, 162848 + 4096 * i, "A") for i in range(2)]
    v_hp = sb("vhp", 128, 32 * 130, BF16, 171040, "A")
    pt_sb = [sb(f"pt{i}", 128, 1024, BF16, 179360 + 2048 * i, "A") for i in range(2)]
    pt_sb2 = [sb(f"ptb{i}", 128, 1024, BF16, 183456 + 2048 * i, "A") for i in range(2)]
    g_sb = sb("gsb", 128, 512, BF16, 183456, "A")
    t1_sb = sb("t1", 128, 512, F32, 184480, "A")
    t2_sb = sb("t2", 128, 512, F32, 186528, "A")
    wqkv_bf = [sb(f"wqkv{i}", 128, DC * 128, BF16, 188576 + 2048 * i, "A")
               for i in range(6)]
    recip2 = sb("recip2", 1, 512, F32, 200864, "A")
    rep2 = sb("rep2", 64, 512, F32, 202912, "A")
    recip_sb = sb("recip", 1, 512, F32, 204960, "A")
    rep_sb = sb("repsb", 64, 512, F32, 207008, "A")
    onesf = sb("onesf", 1, 64, F32, 209056, "A")
    attn_oT = sb("attnoT", 128, DC * OWN, BF16, 23584, "A")
    x2_sb = sb("x2", 128, NT * D, F32, 56352, "J")
    xst = [sb(f"xst{i}", 128, D, F32, 89120 + 4096 * i, "J") for i in range(2)]

    PA = nc.alloc_psum_tensor("psA", [128, 1024], F32)
    PB = nc.alloc_psum_tensor("psB", [128, 1024], F32)
    PC = nc.alloc_psum_tensor("psC", [128, 1024], F32)
    PD = nc.alloc_psum_tensor("psD", [128, 1024], F32)
    PBv = [PB[:, 0:512].bitcast(BF16), PB[:, 512:1024].bitcast(BF16)]

    def dma(q, dst, src, waits=()):
        sem = "q_sp" if q == "sync" else "q_pl"
        return em.op(q, lambda e, d=dst, s=src: e.dma_start(d, s),
                     waits=waits, inc=(sem, 16))

    def vop(build, waits=()):
        return em.op("vector", build, waits=waits, inc=("dve", 1))

    def sop(build, waits=()):
        return em.op("scalar", build, waits=waits, inc=("act", 1))

    def top(build, waits=(), inc=True):
        return em.op("tensor", build, waits=waits,
                     inc=("pe", 1) if inc else None)

    def x_view(t):
        return t[:].rearrange("p (tt d) -> p tt d", tt=NT)

    def xdram_half(t, half):
        return t[half * HALF:(half + 1) * HALF].rearrange("(tt p) d -> p tt d", p=128)

    c_ident = dma("sync", ident[:], ident_i[:])
    dma("sync", prot[:], prot_i[:])
    c_prot = em.cnt["q_sp"]
    dma("sync", sel[:], sel_i[:])
    dma("sync", ones[:], ones_i[:])
    dma("sync", nonce_sb[:], nonce_i[0:1, 0:8])
    dma("sync", flagv_sb[:], flagv_i[:])
    c_bupt = dma("sync", bup_t[:], bup[:, :])
    c_consts = em.cnt["q_sp"]
    c_wg = dma("gpsimd", wg_sb[:].rearrange("p (dc e) -> p dc e", dc=DC),
               Wg.rearrange("(dc p) e -> p dc e", p=128))
    c_bq = dma("gpsimd", qkvp_b[:], bqkvp[:])
    c_bdn = dma("gpsimd", bdn_bf[:], bdn[:, :])
    c_x0 = dma("sync", x_view(x_sb), xdram_half(xin, 0))
    c_negc = vop(lambda v: v.memset(negc, SHIFT))
    c_idf = vop(lambda v: v.tensor_copy(identf[:], ident[:]),
                waits=[("q_sp", c_ident)])
    bup_ev = {}
    for hc in range(HC):
        reg = PA[:, 0:512] if hc % 2 == 0 else PA[:, 512:1024]
        w = [("q_sp", c_bupt), ("dve", c_idf)]
        if hc >= 2:
            w.append(("dve", bup_ev[hc - 2]))
        tb = top(lambda t, hc=hc, reg=reg: t.transpose(
            reg[:, 0:12], bup_t[0:12, hc * 128:(hc + 1) * 128],
            identf[0:12, 0:12]), waits=w)
        bup_ev[hc] = vop(lambda v, hc=hc, reg=reg: v.tensor_copy(
            bup_sb[:, hc * NE:(hc + 1) * NE], reg[:, 0:12]),
            waits=[("pe", tb)])
    setup_done_dve = em.cnt["dve"]
    setup_done_pe = em.cnt["pe"]

    PAv8 = PA[:, 0:512].bitcast(BF16)
    PBv8 = PB[:, 0:512].bitcast(BF16)
    m_all = lnst[:, 0:8]
    rstd_all = lnst[:, 8:16]
    v_all = lnst[:, 16:24]

    def emit_ln(xbuf, xload_tok, dest_view, xnt_pair, sq_s):
        psum_guard = ("act", em.cnt["act"])
        last = 0
        for ts in range(NT):
            xt = xbuf[:, ts * D:(ts + 1) * D]
            c1 = vop(lambda v, xt=xt, ts=ts: v.reduce_sum(
                lnst[:, ts:ts + 1], xt, axis=AX.X), waits=list(xload_tok))
            csq = vop(lambda v, xt=xt: v.tensor_mul(sq_s[:], xt, xt),
                      waits=[("dve", last)] if last else list(xload_tok))
            last = vop(lambda v, ts=ts: v.reduce_sum(
                lnst[:, 16 + ts:17 + ts], sq_s[:], axis=AX.X),
                waits=[("dve", csq)])
        c4 = vop(lambda v: v.tensor_scalar_mul(m_all, m_all, 1.0 / D),
                 waits=[("dve", last)])
        c5 = vop(lambda v: v.tensor_scalar_mul(v_all, v_all, 1.0 / D),
                 waits=[("dve", c4)])
        c6 = vop(lambda v: v.tensor_mul(rstd_all, m_all, m_all),
                 waits=[("dve", c5)])
        c7 = vop(lambda v: v.tensor_tensor(v_all, v_all, rstd_all, ALU.subtract),
                 waits=[("dve", c6)])
        c8 = vop(lambda v: v.tensor_scalar_add(v_all, v_all, 1e-5),
                 waits=[("dve", c7)])
        c9 = vop(lambda v: v.reciprocal(rstd_all, v_all), waits=[("dve", c8)])
        c10 = sop(lambda s: s.activation(rstd_all, rstd_all, AF.Sqrt),
                  waits=[("dve", c9)])
        tr_ev = {}
        tr_sv = {}
        norm_ev = {}
        last_xread = 0
        for ts in range(NT):
            xt = xbuf[:, ts * D:(ts + 1) * D]
            xnt_s = xnt_pair[ts % 2]
            wn = [("act", c10)]
            if ts >= 2:
                wn.append(("pe", tr_ev[ts - 2]))
            norm_ev[ts] = vop(lambda v, xt=xt, ts=ts, xnt_s=xnt_s: v.tensor_scalar(
                xnt_s[:], xt, lnst[:, ts:ts + 1], lnst[:, 8 + ts:9 + ts],
                ALU.subtract, ALU.mult), waits=wn)
            last_xread = norm_ev[ts]
            bank = PAv8 if ts % 2 == 0 else PBv8
            for dcc in range(DC):
                w = [("dve", norm_ev[ts]), ("q_sp", c_ident)]
                if dcc == 0:
                    w.append(psum_guard if ts < 2 else ("act", tr_sv[ts - 2]))
                top(lambda t, dcc=dcc, bank=bank, xnt_s=xnt_s: t.transpose(
                    bank[:, dcc * 128:(dcc + 1) * 128],
                    xnt_s[:, dcc * 128:(dcc + 1) * 128],
                    ident[:]), waits=w if dcc == 0 else (),
                    inc=(dcc == DC - 1))
            tr_ev[ts] = em.cnt["pe"]
            dv = dest_view(ts)
            tr_sv[ts] = sop(lambda s, dv=dv, bank=bank: s.copy(
                dv, bank[:].rearrange("p (dc n) -> p dc n", dc=DC)),
                waits=[("pe", tr_ev[ts])])
        return tr_sv[NT - 1], last_xread

    def emit_gate(xnT_tok):
        ev = {}
        psum_guard = ("dve", em.cnt["dve"])
        post = {}
        last_exp = 0
        for ts in range(NT):
            reg = PC[:, 0:512] if ts % 2 == 0 else PC[:, 512:1024]
            for dcc in range(DC):
                w = []
                if dcc == 0:
                    w = [("act", xnT_tok), ("q_pl", c_wg)]
                    if ts >= 2:
                        w.append(("dve", ev[ts - 2]))
                    else:
                        w.append(psum_guard)
                top(lambda t, ts=ts, dcc=dcc, reg=reg: t.matmul(
                    reg[:, 0:NE],
                    xnT[:, dcc * HALF + ts * 128: dcc * HALF + (ts + 1) * 128],
                    wg_sb[:, dcc * NE:(dcc + 1) * NE],
                    start=(dcc == 0), stop=(dcc == DC - 1)),
                    waits=w, inc=(dcc == DC - 1))
            tgate = em.cnt["pe"]
            sl = (ts % 4) * NE
            c0 = vop(lambda v, reg=reg: v.tensor_copy(lg_t[:], reg[:, 0:NE]),
                     waits=[("pe", tgate)])
            c1 = vop(lambda v: v.reduce_max(m1_t, lg_t[:], axis=AX.X),
                     waits=[("dve", c0)])
            c2 = vop(lambda v: v.tensor_single_scalar(msk_t[:], lg_t[:], m1_t,
                                                      ALU.is_lt),
                     waits=[("dve", c1)])
            c3 = vop(lambda v: v.tensor_scalar(tmp_t[:], msk_t[:], 1e30, -1e30,
                                               ALU.mult, ALU.add),
                     waits=[("dve", c2)])
            c4 = vop(lambda v: v.tensor_mul(l2_t[:], lg_t[:], msk_t[:]),
                     waits=[("dve", c2)])
            c5 = vop(lambda v: v.tensor_add(l2_t[:], l2_t[:], tmp_t[:]),
                     waits=[("dve", c3), ("dve", c4)])
            c6 = vop(lambda v: v.reduce_max(mm_t, l2_t[:], axis=AX.X),
                     waits=[("dve", c5)])
            c7 = vop(lambda v: v.tensor_single_scalar(msk_t[:], l2_t[:], mm_t,
                                                      ALU.is_lt),
                     waits=[("dve", c6)])
            c8 = vop(lambda v: v.tensor_scalar(tmp_t[:], msk_t[:], 1e30, -1e30,
                                               ALU.mult, ALU.add),
                     waits=[("dve", c7)])
            c9 = vop(lambda v: v.tensor_mul(l2_t[:], l2_t[:], msk_t[:]),
                     waits=[("dve", c7)])
            c10 = vop(lambda v: v.tensor_add(l2_t[:], l2_t[:], tmp_t[:]),
                      waits=[("dve", c8), ("dve", c9)])
            c11 = vop(lambda v: v.reduce_max(mm_t, l2_t[:], axis=AX.X),
                      waits=[("dve", c10)])
            wex = [("act", last_exp)] if last_exp else []
            c12 = vop(lambda v, sl=sl: v.tensor_single_scalar(
                sel_all[:, sl:sl + NE], lg_t[:], mm_t, ALU.is_ge),
                waits=[("dve", c11)] + wex)
            c13 = vop(lambda v, sl=sl: v.tensor_single_scalar(
                lg2_all[:, sl:sl + NE], lg_t[:], m1_t, ALU.subtract),
                waits=[("dve", c1)] + wex)
            ev[ts] = max(c12, c13)
            if ts % 4 == 3:
                ce = sop(lambda s: s.activation(lg2_all[:], lg2_all[:], AF.Exp),
                         waits=[("dve", max(ev[ts - 3], ev[ts - 2],
                                            ev[ts - 1], ev[ts]))])
                last_exp = ce
                for t2 in range(ts - 3, ts + 1):
                    sl2 = (t2 % 4) * NE
                    d1 = vop(lambda v, sl2=sl2: v.tensor_tensor(
                        lg2_all[:, sl2:sl2 + NE], lg2_all[:, sl2:sl2 + NE],
                        sel_all[:, sl2:sl2 + NE], ALU.mult),
                        waits=[("act", ce)])
                    d2 = vop(lambda v, sl2=sl2: v.reduce_sum(
                        mm_t, lg2_all[:, sl2:sl2 + NE], axis=AX.X),
                        waits=[("dve", d1)])
                    d3 = vop(lambda v: v.reciprocal(rs_t, mm_t),
                             waits=[("dve", d2)])
                    d4 = vop(lambda v: v.tensor_scalar_mul(rs_t, rs_t, 0.5),
                             waits=[("dve", d3)])
                    post[t2] = vop(lambda v, t2=t2, sl2=sl2: v.tensor_single_scalar(
                        cw_all[:, t2 * NE:(t2 + 1) * NE],
                        lg2_all[:, sl2:sl2 + NE], rs_t, ALU.mult),
                        waits=[("dve", d4)])
        ccast = vop(lambda v: v.tensor_copy(cw_bf[:], cw_all[:]),
                    waits=[("dve", post[NT - 1])])
        w0 = [("dve", ccast), ("act", xnT_tok)]
        for ts in range(NT):
            top(lambda t, ts=ts: t.transpose(
                PAv8[0:12, ts * 128:(ts + 1) * 128],
                cw_bf[:, ts * NE:(ts + 1) * NE], ident[:]),
                waits=w0 if ts == 0 else (), inc=(ts == NT - 1))
        tcw = em.cnt["pe"]
        return vop(lambda v: v.tensor_copy(cwT_bf[0:12, :], PAv8[0:12, :]),
                   waits=[("pe", tcw)])

    state = {"last_dn_evac": 0, "last_dn_evac_PD": 0, "wup_last": [0, 0],
             "gelu_last": [0, 0], "dn_evac6": 0}

    def emit_cwrep(e, cw_tok, pc_guard):
        w0 = [("dve", cw_tok), ("q_sp", c_consts)]
        if pc_guard:
            w0.append(("dve", pc_guard))
        top(lambda t, e=e: t.matmul(
            PC[:, 0:512], sel[:, e * 128:(e + 1) * 128], cwT_bf[:, 0:512],
            start=True, stop=True), waits=w0, inc=False)
        t2 = top(lambda t, e=e: t.matmul(
            PC[:, 512:1024], sel[:, e * 128:(e + 1) * 128],
            cwT_bf[:, 512:1024], start=True, stop=True))
        return vop(lambda v: v.tensor_copy(cwrep[:], PC[:]),
                   waits=[("pe", t2)])

    def emit_ffn_half(xbuf, cw_tok, xnT_tok, extra_wdn_wait=(),
                      extra_wup_wait=()):
        cwrep_tok = emit_cwrep(0, cw_tok, state["last_dn_evac"])
        for e in range(NE):
            up_tok = {}
            gelu_tok = {}
            cwm_tok = {}
            wdn_tok = None
            wdn_v = wdn_sb[:].rearrange("p (hc d) -> p hc d", hc=HC)
            for b_ in range(16):
                wdma = []
                if b_ >= 2:
                    wdma.append(("pe", up_tok[(b_ - 2) * 2 + 1]))
                else:
                    if e == 0:
                        wdma += list(extra_wup_wait)
                    if state["wup_last"][b_ % 2]:
                        wdma.append(("pe", state["wup_last"][b_ % 2]))
                wblk = dma(
                    "gpsimd",
                    wup_bf[b_ % 2][:].rearrange("p (dc h) -> p dc h", dc=DC),
                    Wup[e, :, b_ * 256:(b_ + 1) * 256].rearrange(
                        "(dc p) h -> p dc h", p=128),
                    waits=wdma)
                for sub in range(2):
                    hc = b_ * 2 + sub
                    ps_ = PA if hc % 2 == 0 else PB
                    for st in range(2):
                        for dcc in range(DC):
                            w2 = []
                            if dcc == 0 and st == 0:
                                w2 = [("q_pl", wblk), ("act", xnT_tok)]
                                if hc >= 2:
                                    w2.append(("act", gelu_tok[hc - 2]))
                                elif state["gelu_last"][hc % 2]:
                                    w2.append(("act",
                                               state["gelu_last"][hc % 2]))
                            top(lambda t, hc=hc, st=st, dcc=dcc, ps_=ps_,
                                b_=b_, sub=sub: t.matmul(
                                    ps_[:, st * 512:(st + 1) * 512],
                                    wup_bf[b_ % 2][:, dcc * 256 + sub * 128:
                                                   dcc * 256 + sub * 128 + 128],
                                    xnT[:, dcc * HALF + st * 512:
                                        dcc * HALF + (st + 1) * 512],
                                    start=(dcc == 0), stop=(dcc == DC - 1)),
                                waits=w2, inc=(dcc == DC - 1 and st == 1))
                    up_tok[hc] = em.cnt["pe"]
                    gelu_tok[hc] = sop(lambda s, hc=hc, ps_=ps_, e=e: s.activation(
                        h_sb[:, hc * HALF:(hc + 1) * HALF], ps_[:], AF.Gelu,
                        bias=bup_sb[:, hc * NE + e: hc * NE + e + 1]),
                        waits=[("pe", up_tok[hc])])
                    cwm_tok[hc] = vop(lambda v, hc=hc: v.tensor_tensor(
                        h_sb[:, hc * HALF:(hc + 1) * HALF],
                        h_sb[:, hc * HALF:(hc + 1) * HALF], cwrep[:], ALU.mult),
                        waits=[("act", gelu_tok[hc]), ("dve", cwrep_tok)])
                if b_ % 2 == 1:
                    ch = (b_ - 1) // 2
                    wdnw = []
                    if ch == 0:
                        wdnw = list(extra_wdn_wait) if e == 0 else []
                        if state["last_dn_evac"]:
                            wdnw.append(("dve", state["last_dn_evac"]))
                    wdn_tok = dma(
                        "gpsimd", wdn_v[:, ch * 4:(ch + 1) * 4],
                        Wdn[e, ch * 512:(ch + 1) * 512, :].rearrange(
                            "(hc p) d -> p hc d", p=128),
                        waits=wdnw)
            state["wup_last"][0] = up_tok[29]
            state["wup_last"][1] = up_tok[31]
            state["gelu_last"][0] = gelu_tok[30]
            state["gelu_last"][1] = gelu_tok[31]
            dn_evac = {}
            this_cwrep = cwrep_tok
            for ts in range(NT):
                psd = PC if ts % 2 == 0 else PD
                wf = [("dve", cwm_tok[HC - 1]), ("q_pl", wdn_tok)]
                if ts >= 2:
                    wf.append(("dve", dn_evac[ts - 2]))
                elif ts == 0:
                    wf.append(("dve", this_cwrep))
                    if state["last_dn_evac"]:
                        wf.append(("dve", state["last_dn_evac"]))
                elif ts == 1 and state["last_dn_evac_PD"]:
                    wf.append(("dve", state["last_dn_evac_PD"]))
                first = True
                if e == 0:
                    for dh in range(2):
                        top(lambda t, ts=ts, dh=dh, psd=psd: t.matmul(
                            psd[:, dh * 512:(dh + 1) * 512],
                            cwT_bf[:, ts * 128:(ts + 1) * 128],
                            bdn_bf[:, dh * 512:(dh + 1) * 512],
                            start=True, stop=False),
                            waits=wf if dh == 0 else (), inc=False)
                    first = False
                for hc in range(HC):
                    for dh in range(2):
                        st_ = (hc == 0 and e != 0)
                        top(lambda t, ts=ts, hc=hc, dh=dh, psd=psd, st_=st_:
                            t.matmul(
                                psd[:, dh * 512:(dh + 1) * 512],
                                h_sb[:, hc * HALF + ts * 128:
                                     hc * HALF + (ts + 1) * 128],
                                wdn_sb[:, hc * D + dh * 512:
                                       hc * D + (dh + 1) * 512],
                                start=st_, stop=(hc == HC - 1)),
                            waits=(wf if (first and hc == 0 and dh == 0) else ()),
                            inc=(hc == HC - 1 and dh == 1))
                tdn = em.cnt["pe"]
                dn_evac[ts] = vop(lambda v, ts=ts, psd=psd: v.tensor_tensor(
                    xbuf[:, ts * D:(ts + 1) * D], xbuf[:, ts * D:(ts + 1) * D],
                    psd[:], ALU.add), waits=[("pe", tdn)])
                if ts % 2 == 1:
                    state["last_dn_evac_PD"] = dn_evac[ts]
                if ts == NT - 2:
                    state["dn_evac6"] = dn_evac[ts]
                if ts == NT - 1 and e + 1 < NE:
                    cwrep_tok = emit_cwrep(e + 1, cw_tok, state["dn_evac6"])
            state["last_dn_evac"] = max(dn_evac[NT - 1], dn_evac[NT - 2])
        return max(dn_evac[NT - 1], dn_evac[NT - 2])

    xnT_v = xnT[:].rearrange("p (dc t) -> p dc t", dc=DC)
    ln_tok, _ = emit_ln(x_sb, [("q_sp", c_x0)],
                        lambda ts: xnT_v[:, :, ts * 128:(ts + 1) * 128],
                        xnt_A, sq_A)
    cw1 = emit_gate(ln_tok)
    ffn_h0 = emit_ffn_half(x_sb, cw1, ln_tok,
                           extra_wdn_wait=[("dve", setup_done_dve),
                                           ("pe", setup_done_pe)])
    c_sp0 = dma("sync", xdram_half(xscr1, 0), x_view(x_sb),
                waits=[("dve", ffn_h0)])
    c_x1 = dma("sync", x_view(x_sb), xdram_half(xin, 1), waits=[("q_sp", c_sp0)])
    ln_tok, _ = emit_ln(x_sb, [("q_sp", c_x1)],
                        lambda ts: xnT_v[:, :, ts * 128:(ts + 1) * 128],
                        xnt_A, sq_A)
    cw1b = emit_gate(ln_tok)
    ffn_h1 = emit_ffn_half(x_sb, cw1b, ln_tok)

    if stage == "ffn1":
        c_o1 = dma("sync", xdram_half(yout, 1), x_view(x_sb),
                   waits=[("dve", ffn_h1)])
        c_o0 = dma("gpsimd", yout[0:HALF, :], xscr1[0:HALF, :],
                   waits=[("q_sp", c_sp0)])
        em.op("sync", lambda e: e.wait_ge(SEM["q_sp"], em.cnt["q_sp"]))
        em.op("sync", lambda e: e.wait_ge(SEM["q_pl"], em.cnt["q_pl"]))
        return finish(nc, em)

    xn1T_v = xn1T[:].rearrange("p (dc j) -> p dc j", dc=DC)
    ln2b_tok, ln2b_xr = emit_ln(
        x_sb, [("dve", ffn_h1)],
        lambda ts: xn1T_v[:, :, HALF + ts * 128: HALF + (ts + 1) * 128],
        xnt_B, sq_B)
    c_sp1 = dma("sync", xdram_half(xscr1, 1), x_view(x_sb),
                waits=[("dve", ffn_h1)])
    c_xr1 = dma("sync", x_view(x_sb), xdram_half(xscr1, 0),
                waits=[("q_sp", c_sp1), ("dve", ln2b_xr)])
    ln2a_tok, ln2a_xr = emit_ln(
        x_sb, [("q_sp", c_xr1)],
        lambda ts: xn1T_v[:, :, ts * 128:(ts + 1) * 128], xnt_B, sq_B)

    xn1T_own = xn1T[:].rearrange("p (dc j) -> p dc j", dc=DC)[:, :, 0:OWN]
    xn1T_par = xn1T[:].rearrange("p (dc j) -> p dc j", dc=DC)[:, :, OWN:L]

    def ex_write(eng):
        eng.wait_ge(SEM["act"], EXW["act"])
        for arm in eng.switch_core_id(128):
            q = int(arm) % 2
            eng.dma_start(xch[q], xn1T_own).then_inc(SEM["q_sp"], 16)
        eng.wait_ge(SEM["q_sp"], EXW["q_sp_after_data"])
        for arm in eng.switch_core_id(128):
            q = int(arm) % 2
            eng.dma_start(Ffl[q:q + 1, :], flagv_sb[0:1, :]).then_inc(SEM["q_sp"], 16)
        eng.wait_ge(SEM["q_sp"], EXW["q_sp_after_flag"])
        tgt = eng.value_load(nonce_sb[0:1, 1:2])
        with eng.register("spin_t") as rt:
            eng.reg_mov(rt, 0)
            eng.br("spin_lp")
            with nc.body("spin_lp", valid_engines=[eng.engine]):
                eng.reg_alu(rt, rt, 16, op=ALU.add)
                eng.dma_start(flag_rd[0:1, :],
                              Ffl.rearrange("a b -> (a b)")[None, :]
                              ).then_inc(SEM["q_spin"], 16)
                eng.wait_ge(SEM["q_spin"], rt)
                v0 = eng.value_load(flag_rd[0:1, 0:1])
                eng.br_ne(v0, tgt, on_true="spin_lp", on_false="spin_c1")
            with nc.body("spin_c1", valid_engines=[eng.engine]):
                v1 = eng.value_load(flag_rd[0:1, 128:129])
                eng.br_ne(v1, tgt, on_true="spin_lp", on_false="spin_dn")
            nc.switch_bb("spin_dn")
        for arm in eng.switch_core_id(128):
            q = int(arm) % 2
            eng.dma_start(xn1T_par, xch[1 - q]).then_inc(SEM["q_sp"], 16)

    EXW = {"act": max(ln2a_tok, ln2b_tok)}
    EXW["q_sp_after_data"] = em.cnt["q_sp"] + 16
    EXW["q_sp_after_flag"] = em.cnt["q_sp"] + 32
    toks = em.raw("sync", ex_write,
                  bump=[("q_sp", 48), ("q_spin", 0)])
    c_xn1T_full = toks["q_sp"]

    if stage == "xn1":
        c_d = dma("gpsimd",
                  yout.rearrange("(p r) d -> p r d", p=128),
                  xn1T[:, 0:16384].rearrange("p (r d) -> p r d", d=1024),
                  waits=[("q_sp", c_xn1T_full)])
        em.op("sync", lambda e: e.wait_ge(SEM["q_sp"], em.cnt["q_sp"]))
        em.op("gpsimd", lambda e: e.wait_ge(SEM["q_pl"], em.cnt["q_pl"]))
        return finish(nc, em)

    attn_pe0 = em.cnt["pe"]
    dma("sync", qcos[:], qcos_i[:], waits=[("pe", attn_pe0)])
    dma("sync", qsin[:], qsin_i[:])
    dma("sync", kcos[:], kcos_i[:])
    c_qtab = c_ktab = dma("sync", ksin[:], ksin_i[:])
    c_vinit = vop(lambda v: v.memset(v_hp[:], 1.0), waits=[("pe", attn_pe0)])
    c_onesf = vop(lambda v: v.memset(onesf[:], 1.0), waits=[("pe", attn_pe0)])

    wq_tok = {}
    att_state = {"av_prev": [0, 0], "last_exp": [0, 0], "last_avpe": [0, 0],
                 "rep_evac": [0, 0], "last_ca": 0, "last_trep": [0, 0], "last_norm": [0, 0]}
    qkv_last_pe = {}

    def emit_qkv(hp):
        bufs = [wqkv_bf[(hp % 2) * 3 + i] for i in range(3)]
        wsrc = [Wq, Wk, Wv]
        dtoks = []
        for i in range(3):
            w = []
            if hp >= 2 and (hp - 2, i) in wq_tok:
                w.append(("pe", wq_tok[(hp - 2, i)]))
            if hp < 2:
                w.append(("pe", attn_pe0))
            dtoks.append(dma(
                "gpsimd",
                bufs[i][:].rearrange("p (dc n) -> p dc n", dc=DC),
                wsrc[i][:, hp * 128:(hp + 1) * 128].rearrange(
                    "(dc p) n -> p dc n", p=128), waits=w))
        hp_guard_dve = ("dve", em.cnt["dve"])
        hp_guard_act = ("act", em.cnt["act"])
        rope_done = []
        for which, nstrip, tabcs, W_i in ((0, 4, (qcos, qsin), 0),
                                          (1, 8, (kcos, ksin), 1)):
            for st in range(nstrip):
                reg = PD[:, 0:512] if st % 2 == 0 else PD[:, 512:1024]
                for dcc in range(DC):
                    w = []
                    if dcc == 0:
                        w = [("q_pl", dtoks[W_i]), ("q_sp", c_xn1T_full),
                             ("q_sp", c_qtab if which == 0 else c_ktab)]
                        if len(rope_done) >= 2:
                            w.append(("act", rope_done[-2][0]))
                        else:
                            w.append(hp_guard_dve)
                    top(lambda t, st=st, dcc=dcc, reg=reg, b_=bufs[W_i]: t.matmul(
                        reg, b_[:, dcc * 128:(dcc + 1) * 128],
                        xn1T[:, dcc * L + st * 512: dcc * L + (st + 1) * 512],
                        start=(dcc == 0), stop=False),
                        waits=w, inc=False)
                bofs = (0 if which == 0 else D) + hp * 128
                top(lambda t, reg=reg, bofs=bofs: t.matmul(
                    reg, qkvp_b[0:1, bofs:bofs + 128], ones[0:1, 0:512],
                    start=False, stop=True),
                    waits=[("q_pl", c_bq), ("q_sp", c_consts)])
                tproj = em.cnt["pe"]
                wce = [("pe", tproj)]
                if att_state["last_ca"]:
                    wce.append(("dve", att_state["last_ca"]))
                ce = sop(lambda s, reg=reg: s.copy(g_sb[:], reg), waits=wce)
                w_rot = [("act", ce), ("q_sp", c_prot)]
                if len(rope_done) >= 1:
                    w_rot.append(("dve", rope_done[-1][1]))
                else:
                    w_rot.append(hp_guard_dve)
                trot = top(lambda t: t.matmul(PC[:, 0:512], prot[:], g_sb[:],
                                              start=True, stop=True),
                           waits=w_rot)
                tc = tabcs[0][:, st * 512:(st + 1) * 512]
                tsn = tabcs[1][:, st * 512:(st + 1) * 512]
                dsts = qT_h if which == 0 else kT_h
                ca = vop(lambda v, tc=tc: v.tensor_tensor(
                    t1_sb[:], g_sb[:], tc, ALU.mult), waits=[("act", ce)])
                att_state["last_ca"] = ca
                cb = vop(lambda v, tsn=tsn: v.tensor_tensor(
                    t2_sb[:], PC[:, 0:512], tsn, ALU.mult), waits=[("pe", trot)])
                lastv = 0
                for hh in range(2):
                    o = hh * 64
                    lastv = vop(lambda v, o=o, hh=hh, st=st, dsts=dsts:
                                v.tensor_add(
                                    dsts[hh][:, st * 512:(st + 1) * 512],
                                    t1_sb[o:o + 64, :], t2_sb[o:o + 64, :]),
                                waits=[("dve", ca), ("dve", cb)])
                rope_done.append((ce, lastv))
        qk_done = rope_done[-1][1]
        v_ev = {}
        for jt in range(32):
            reg = PD[:, 0:512] if jt % 2 == 0 else PD[:, 512:1024]
            for dcc in range(DC):
                w = []
                if dcc == 0:
                    w = [("q_pl", dtoks[2]), ("act", rope_done[-1][0])]
                    if jt >= 2:
                        w.append(("dve", v_ev[jt - 2]))
                top(lambda t, jt=jt, dcc=dcc, reg=reg: t.matmul(
                    reg[:, 0:128], xn1T[:, dcc * L + jt * 128: dcc * L + (jt + 1) * 128],
                    bufs[2][:, dcc * 128:(dcc + 1) * 128],
                    start=(dcc == 0), stop=False), waits=w, inc=False)
            top(lambda t, reg=reg, hp=hp: t.matmul(
                reg[:, 0:128], ones[0:1, 0:128],
                qkvp_b[0:1, 2 * D + hp * 128: 2 * D + hp * 128 + 128],
                start=False, stop=True), waits=[])
            tv = em.cnt["pe"]
            cva = vop(lambda v, jt=jt, reg=reg: v.tensor_copy(
                v_hp[:, jt * 130:jt * 130 + 64], reg[:, 0:64]),
                waits=[("pe", tv), ("dve", c_vinit)])
            v_ev[jt] = vop(lambda v, jt=jt, reg=reg: v.tensor_copy(
                v_hp[:, jt * 130 + 65:jt * 130 + 129], reg[:, 64:128]),
                waits=[("dve", cva)])
        qkv_last_pe[hp] = em.cnt["pe"]
        wq_tok[(hp, 0)] = qkv_last_pe[hp]
        wq_tok[(hp, 1)] = qkv_last_pe[hp]
        wq_tok[(hp, 2)] = qkv_last_pe[hp]
        return qk_done, v_ev[31], rope_done[-1][0]

    def emit_scores_av_pair(hp, qk_tok, v_tok):
        dcc = hp
        pts = [pt_sb, pt_sb2]
        sregs = [PA, PB]
        for ist in range(4):
            avreg = [PC[0:65, 0:512], PC[0:65, 512:1024]]
            exp_tok = {}
            av_tok = {}
            for grp in range(16):
                for hh in range(2):
                    w = []
                    if grp == 0:
                        w = [("dve", qk_tok), ("dve", v_tok)]
                        if att_state["av_prev"][hh]:
                            w.append(("dve", att_state["av_prev"][hh]))
                        if att_state["last_exp"][hh]:
                            w.append(("act", att_state["last_exp"][hh]))
                        if max(att_state["rep_evac"]):
                            w.append(("act", max(att_state["rep_evac"])))
                    else:
                        w.append(("act", exp_tok[(grp - 1, hh)]))
                    for sub in range(2):
                        jt = grp * 2 + sub
                        top(lambda t, jt=jt, sreg=sregs[hh], ist=ist, hh=hh,
                            sub=sub: t.matmul(
                                sreg[:, sub * 512:(sub + 1) * 512],
                                kT_h[hh][:, jt * 128:(jt + 1) * 128],
                                qT_h[hh][:, ist * 512:(ist + 1) * 512],
                                start=True, stop=True),
                            waits=w if sub == 0 else (), inc=(sub == 1))
                    ts_tok = em.cnt["pe"]
                    wexp = [("pe", ts_tok), ("dve", c_negc)]
                    if grp >= 2:
                        wexp.append(("pe", av_tok[(grp - 2, hh)]))
                    elif att_state["last_avpe"][hh]:
                        wexp.append(("pe", att_state["last_avpe"][hh]))
                    if grp == 0:
                        wexp.append(("dve", qk_tok))
                    exp_tok[(grp, hh)] = sop(
                        lambda s, sreg=sregs[hh], ptb=pts[hh][grp % 2]:
                        s.activation(ptb[:], sreg[:], AF.Exp, bias=negc,
                                     scale=0.125), waits=wexp)
                for hh in range(2):
                    if grp == 0:
                        continue
                    g2 = grp - 1
                    for sub in range(2):
                        jt = g2 * 2 + sub
                        top(lambda t, jt=jt, sub=sub, hh=hh, g2=g2,
                            ptb=pts[hh][g2 % 2]: t.matmul(
                                avreg[hh],
                                v_hp[:, jt * 130 + hh * 65:
                                     jt * 130 + hh * 65 + 65],
                                ptb[:, sub * 512:(sub + 1) * 512],
                                start=(g2 == 0 and sub == 0), stop=False),
                            waits=[("act", exp_tok[(g2, hh)])]
                            if sub == 0 else (), inc=(sub == 1))
                    av_tok[(g2, hh)] = em.cnt["pe"]
            for hh in range(2):
                g2 = 15
                for sub in range(2):
                    jt = g2 * 2 + sub
                    top(lambda t, jt=jt, sub=sub, hh=hh,
                        ptb=pts[hh][g2 % 2]: t.matmul(
                            avreg[hh],
                            v_hp[:, jt * 130 + hh * 65: jt * 130 + hh * 65 + 65],
                            ptb[:, sub * 512:(sub + 1) * 512],
                            start=False, stop=(sub == 1)),
                        waits=[("act", exp_tok[(g2, hh)])] if sub == 0 else (),
                        inc=(sub == 1))
                av_tok[(15, hh)] = em.cnt["pe"]
            for hh in range(2):
                rcp = recip_sb if hh == 0 else recip2
                repb = rep_sb if hh == 0 else rep2
                wcr = [("pe", av_tok[(15, hh)])]
                if att_state["last_trep"][hh]:
                    wcr.append(("pe", att_state["last_trep"][hh]))
                cr = vop(lambda v, hh=hh, rcp=rcp: v.reciprocal(
                    rcp[:], avreg[hh][64:65, :]), waits=wcr)
                wrep = [("dve", cr), ("dve", c_onesf)]
                if att_state["rep_evac"][hh]:
                    wrep.append(("act", att_state["rep_evac"][hh]))
                pdslot = PD[0:64, 0:512] if hh == 0 else PD[0:64, 512:1024]
                trep = top(lambda t, rcp=rcp, pdslot=pdslot: t.matmul(
                    pdslot, onesf[0:1, :], rcp[0:1, :],
                    start=True, stop=True), waits=wrep)
                att_state["last_trep"][hh] = trep
                wre = [("pe", trep)]
                if att_state["last_norm"][hh]:
                    wre.append(("dve", att_state["last_norm"][hh]))
                att_state["rep_evac"][hh] = sop(
                    lambda s, repb=repb, pdslot=pdslot: s.copy(repb[:], pdslot),
                    waits=wre)
                att_state["av_prev"][hh] = vop(
                    lambda v, dcc=dcc, hh=hh, ist=ist, repb=repb: v.tensor_tensor(
                        attn_oT[hh * 64:(hh + 1) * 64,
                                dcc * OWN + ist * 512: dcc * OWN + (ist + 1) * 512],
                        avreg[hh][0:64, :], repb[:], ALU.mult),
                    waits=[("act", att_state["rep_evac"][hh])])
                att_state["last_norm"][hh] = att_state["av_prev"][hh]
                att_state["last_exp"][hh] = exp_tok[(15, hh)]
                att_state["last_avpe"][hh] = av_tok[(15, hh)]
        return att_state["av_prev"]

    attn_done_tok = 0
    for hp in range(8):
        qk_tok, v_tok, _ = emit_qkv(hp)
        emit_scores_av_pair(hp, qk_tok, v_tok)
    attn_done_tok = max(att_state["av_prev"])

    if stage == "attn":
        c_d = dma("gpsimd", yout.rearrange("(p r) d -> p r d", p=128),
                  attn_oT[:].rearrange("p (r d) -> p r d", d=1024),
                  waits=[("dve", attn_done_tok)])
        em.op("sync", lambda e: e.wait_ge(SEM["q_sp"], em.cnt["q_sp"]))
        em.op("gpsimd", lambda e: e.wait_ge(SEM["q_pl"], em.cnt["q_pl"]))
        return finish(nc, em)

    wp_full = sb("wpfull", 128, DC * D, BF16, 146464, "J")
    attn_last_pe = em.cnt["pe"]
    c_wp = dma("gpsimd", wp_full[:].rearrange("p (dc n) -> p dc n", dc=DC),
               Wp.rearrange("(dc p) n -> p dc n", p=128),
               waits=[("pe", attn_last_pe)])
    proj_spills = []
    for halfp in range(2):
        proj_evac = {}
        for ts in range(NT):
            reg = PA if ts % 2 == 0 else PB
            wxr = [("dve", attn_done_tok)]
            if ts >= 2:
                wxr.append(("dve", proj_evac[ts - 2]))
            xstb = xst[ts % 2]
            cxr = dma("sync", xstb[:],
                      xscr1[halfp * HALF + ts * 128:
                            halfp * HALF + (ts + 1) * 128, :],
                      waits=wxr)
            for dh in range(2):
                for dcc in range(DC):
                    w = []
                    if dcc == 0:
                        w = [("q_pl", c_wp), ("dve", attn_done_tok),
                             ("act", max(att_state["last_exp"]))]
                        if ts >= 2:
                            w.append(("dve", proj_evac[ts - 2]))
                    top(lambda t, ts=ts, dh=dh, dcc=dcc, reg=reg, halfp=halfp:
                        t.matmul(
                            reg[:, dh * 512:(dh + 1) * 512],
                            attn_oT[:, dcc * OWN + halfp * HALF + ts * 128:
                                    dcc * OWN + halfp * HALF + (ts + 1) * 128],
                            wp_full[:, dcc * D + dh * 512: dcc * D + (dh + 1) * 512],
                            start=(dcc == 0), stop=False),
                        waits=w, inc=False)
                top(lambda t, reg=reg, dh=dh: t.matmul(
                    reg[:, dh * 512:(dh + 1) * 512], ones[0:1, 0:128],
                    qkvp_b[0:1, 3 * D + dh * 512: 3 * D + (dh + 1) * 512],
                    start=False, stop=True), waits=[])
            tpj = em.cnt["pe"]
            proj_evac[ts] = vop(lambda v, ts=ts, reg=reg, xstb=xstb:
                                v.tensor_tensor(
                                    x2_sb[:, ts * D:(ts + 1) * D], xstb[:],
                                    reg[:], ALU.add),
                                waits=[("pe", tpj), ("q_sp", cxr)])
        proj_spills.append(dma(
            "sync", xdram_half(xscr2, halfp), x_view(x2_sb),
            waits=[("dve", proj_evac[NT - 1]), ("dve", proj_evac[NT - 2])]))
    proj_all_done = proj_spills[-1]

    if stage == "x2":
        c_o0 = dma("gpsimd", yout[:, :], xscr2[:, :], waits=[("q_sp", proj_all_done)])
        em.op("sync", lambda e: e.wait_ge(SEM["q_sp"], em.cnt["q_sp"]))
        em.op("gpsimd", lambda e: e.wait_ge(SEM["q_pl"], em.cnt["q_pl"]))
        return finish(nc, em)

    state["last_dn_evac"] = 0
    state["last_dn_evac_PD"] = 0
    prev_out = 0
    for halfp in range(2):
        wcx = [("q_sp", proj_all_done)]
        if halfp:
            wcx += [("dve", state["last_dn_evac"]), ("q_sp", prev_out)]
        else:
            wcx.append(("pe", em.cnt["pe"]))
        cx = dma("sync", x_view(x_sb), xdram_half(xscr2, halfp), waits=wcx)
        ln_tok3, _ = emit_ln(x_sb, [("q_sp", cx)],
                             lambda ts: xnT_v[:, :, ts * 128:(ts + 1) * 128],
                             xnt_A, sq_A)
        cw3 = emit_gate(ln_tok3)
        extra = ([("pe", em.cnt["pe"])], [("pe", em.cnt["pe"])]) if halfp == 0 else ((), ())
        ffn2 = emit_ffn_half(x_sb, cw3, ln_tok3, extra_wdn_wait=extra[0],
                             extra_wup_wait=extra[1])
        prev_out = dma("sync", xdram_half(yout, halfp), x_view(x_sb),
                       waits=[("dve", ffn2)])

    em.op("sync", lambda e: e.wait_ge(SEM["q_sp"], em.cnt["q_sp"]))
    em.op("gpsimd", lambda e: e.wait_ge(SEM["q_pl"], em.cnt["q_pl"]))
    return finish(nc, em)


SEM = {}


def finish(nc, em):
    global SEM
    SEM = {s: nc.alloc_semaphore("sem_" + s) for s in em.cnt}
    with nc.Block() as blk:
        for eng_name in Em.ENGS:
            evs = em.ev[eng_name]
            if not evs:
                continue

            def make(evs):
                def emit_all(eng):
                    for build, waits, inc in evs:
                        if waits is None:
                            build(eng)
                            continue
                        for s, v in waits:
                            if v:
                                eng.wait_ge(SEM[s], v)
                        ins = build(eng)
                        if inc is not None and ins is not None:
                            ins.then_inc(SEM[inc[0]], inc[1])
                return emit_all

            getattr(blk, eng_name)(make(evs))
    return nc


def _shard_inputs(inputs):
    x = np.ascontiguousarray(np.asarray(inputs["L0_attn_x"], dtype=np.float32))
    consts = {
        "identi": np.eye(128, dtype=np.float32).astype(BF),
        "proti": _prot_matrix(),
        "seli": _sel_matrix(),
        "onesi": np.ones((1, 512), dtype=np.float32).astype(BF),
    }
    nonce = int(np.random.randint(1, 1 << 28))
    nvec = (np.arange(128, dtype=np.int64) + nonce).astype(np.int32)
    shared = {
        "Wq": np.ascontiguousarray(inputs["L0_attn_Wq"], dtype=np.float32),
        "Wk": np.ascontiguousarray(inputs["L0_attn_Wk"], dtype=np.float32),
        "Wv": np.ascontiguousarray(inputs["L0_attn_Wv"], dtype=np.float32),
        "Wp": np.ascontiguousarray(inputs["L0_attn_Wp"], dtype=np.float32),
        "bqkvp": np.concatenate([
            np.asarray(inputs["L0_attn_bq"], dtype=np.float32),
            np.asarray(inputs["L0_attn_bk"], dtype=np.float32),
            np.asarray(inputs["L0_attn_bv"], dtype=np.float32),
            np.asarray(inputs["L0_attn_bp"], dtype=np.float32)])[None, :],
        "Wg": np.ascontiguousarray(inputs["L0_ffn_Wg"], dtype=np.float32),
        "Wup": np.ascontiguousarray(inputs["L0_ffn_Wup"], dtype=np.float32),
        "bupb": np.ascontiguousarray(inputs["L0_ffn_bup"], dtype=np.float32),
        "Wdn": np.ascontiguousarray(inputs["L0_ffn_Wdown"], dtype=np.float32),
        "bdnb": np.ascontiguousarray(inputs["L0_ffn_bdown"], dtype=np.float32),
        "noncei": nvec[None, :],
        "flagvi": np.full((1, 128), nonce + 1, dtype=np.int32),
        **consts,
    }
    in_maps = []
    for c in range(NCORES):
        b, hh = c // 2, c % 2
        own_pos = np.arange(hh * OWN, (hh + 1) * OWN)
        all_pos = np.concatenate([own_pos, np.arange((1 - hh) * OWN,
                                                     (2 - hh) * OWN)])
        m = dict(shared)
        m["xin"] = np.ascontiguousarray(x[b, hh * OWN:(hh + 1) * OWN, :])
        qt = _rope_tab(own_pos)
        kt = _rope_tab(all_pos)
        m["qcosi"] = np.tile(qt[0:64], (2, 1)).astype(BF)
        m["qsini"] = np.tile(qt[64:128], (2, 1)).astype(BF)
        m["kcosi"] = np.tile(kt[0:64], (2, 1)).astype(BF)
        m["ksini"] = np.tile(kt[64:128], (2, 1)).astype(BF)
        in_maps.append(m)
    return in_maps


def kernel(**inputs):
    in_maps = _shard_inputs(inputs)
    nc = build_program("full")
    res = run_bass_kernel_spmd(nc, in_maps, core_ids=list(range(NCORES)))
    out = np.empty((4, L, D), dtype=np.float32)
    for c in range(NCORES):
        b, hh = c // 2, c % 2
        out[b, hh * OWN:(hh + 1) * OWN, :] = res.results[c]["yout"]
    return out
